# version 02
# baseline (speedup 1.0000x reference)
import numpy as np
import ml_dtypes
from contextlib import ExitStack

import concourse.bass as bass
import concourse.mybir as mybir
from concourse.bass_utils import run_bass_kernel_spmd

F32 = mybir.dt.float32
BF16 = mybir.dt.bfloat16
I32 = mybir.dt.int32
AF = mybir.ActivationFunctionType
ALU = mybir.AluOpType
AX = mybir.AxisListType
BF16NP = ml_dtypes.bfloat16

NCORES = 8
D = 4096
NC = 32
TCORE = 1024
T2 = 512
NT2 = 2
NH = 32
NKVH = 8
DH = 128
DKV = NKVH * DH
RANK = 32
NEXP = 8
LN_EPS = 1e-5
G_EPS = 1e-6
THETA = 10000.0
PI = float(np.pi)
SEQ = 2048

EVEN_TILES = [0, 1, 4, 5, 8, 9, 12, 13]
ODD_TILES = [2, 3, 6, 7, 10, 11, 14, 15]


def core_tiles(c):
    return EVEN_TILES if c % 2 == 0 else ODD_TILES


def core_token_idx(c):
    return np.concatenate([np.arange(t * 128, (t + 1) * 128) for t in core_tiles(c)])



ENGS = ("sync", "scalar", "vector", "tensor", "gpsimd")


class Prog:

    def __init__(self, nc, stack):
        self.nc = nc
        self.stack = stack
        self.ops = {e: [] for e in ENGS}
        self.sems = {}
        self.cnt = {}
        self.last_wait = {}

    def sem(self, name):
        if name not in self.sems:
            self.sems[name] = self.stack.enter_context(self.nc.semaphore(name))
            self.cnt[name] = 0
        return self.sems[name]

    def wait(self, eng, ev):
        if ev is None:
            return
        s, v = ev
        if v <= 0:
            return
        key = (eng, s)
        if self.last_wait.get(key, 0) >= v:
            return
        self.last_wait[key] = v
        h = self.sem(s)
        self.ops[eng].append(lambda e, h=h, v=v: e.wait_ge(h, v))

    def op(self, eng, fn, inc=None, waits=(), inc_by=1):
        for ev in waits:
            self.wait(eng, ev)
        ev = None
        if inc is not None:
            h = self.sem(inc)
            self.cnt[inc] += inc_by
            ev = (inc, self.cnt[inc])
            self.ops[eng].append(
                lambda e, fn=fn, h=h, k=inc_by: fn(e).then_inc(h, k))
        else:
            self.ops[eng].append(fn)
        return ev

    def dma(self, eng, out, in_, inc, waits=()):
        return self.op(eng, lambda e: e.dma_start(out=out, in_=in_),
                       inc=inc, waits=waits, inc_by=16)

    def ev(self, semname):
        return (semname, self.cnt.get(semname, 0))

    def emit(self):
        blk = self.stack.enter_context(self.nc.Block())
        for eng in ENGS:
            lst = self.ops[eng]
            if not lst:
                continue

            def body(e, lst=lst):
                for f in lst:
                    f(e)

            getattr(blk, eng)(body)




def emit_layernorm(p, B, r_ready_ev, prev, xln_dma=None):
    casts = []
    sq_evs = []
    for c in range(NC):
        ev = p.op("scalar", lambda s, c=c: s.activation(
            out=B["xb"][:, c * T2:(c + 1) * T2],
            in_=B["rbuf"][:, c * T2:(c + 1) * T2], func=AF.Copy),
            inc="sa", waits=[r_ready_ev, prev.get("pe"), prev.get("so")])
        g = sq_evs[c - 2][1] if c >= 2 else prev.get("pe")
        ev2 = p.op("scalar", lambda s, c=c: s.activation(
            out=B["x2"][c % 2][:], in_=B["rbuf"][:, c * T2:(c + 1) * T2],
            func=AF.Square), inc="sa", waits=[g])
        casts.append((ev, ev2))
        w0 = [ev2] + ([prev.get("dve"), B.get("cev")] if c == 0 else [])
        p.op("tensor", lambda t, c=c: t.matmul(
            B["ps_s"][:], B["ones1"][:], B["xb"][:, c * T2:(c + 1) * T2],
            start=(c == 0), stop=(c == NC - 1)), waits=w0)
        e_mm = p.op("tensor", lambda t, c=c: t.matmul(
            B["ps_q"][:], B["ones1"][:], B["x2"][c % 2][:],
            start=(c == 0), stop=(c == NC - 1)), inc="sp")
        sq_evs.append((None, e_mm))
    stats_done = p.ev("sp")

    st = B["stat"]
    m_, msq, sd, rstd = (st[:, i * T2:(i + 1) * T2] for i in range(4))
    var = msq
    e1 = p.op("vector", lambda v: v.tensor_scalar_mul(m_, B["ps_s"][:], 1.0 / D),
              inc="sv", waits=[stats_done])
    e2 = p.op("vector", lambda v: v.tensor_scalar_mul(msq, B["ps_q"][:], 1.0 / D),
              inc="sv", waits=[e1])
    e3 = p.op("vector", lambda v: v.tensor_mul(sd, m_, m_), inc="sv", waits=[e2])
    e4 = p.op("vector", lambda v: v.tensor_sub(var, msq, sd), inc="sv", waits=[e3])
    e5 = p.op("scalar", lambda s: s.activation(out=sd, in_=var, func=AF.Sqrt,
                                               bias=B["epst"][:]),
              inc="sa", waits=[e4])
    e6 = p.op("vector", lambda v: v.reciprocal(rstd, sd), inc="sv", waits=[e5])
    e7 = p.op("tensor", lambda t: t.matmul(B["ps_b"][:], B["onesf"][:], m_,
                                           start=True, stop=True),
              inc="sp", waits=[e6, prev.get("dve"), B.get("cev")])
    e8 = p.op("vector", lambda v: v.tensor_copy(B["mb"][:], B["ps_b"][:]),
              inc="sv", waits=[e7, prev.get("pool"), prev.get("dve")])
    e9 = p.op("tensor", lambda t: t.matmul(B["ps_b"][:], B["onesf"][:], rstd,
                                           start=True, stop=True),
              inc="sp", waits=[e8])
    e10 = p.op("vector", lambda v: v.tensor_copy(B["rb"][:], B["ps_b"][:]),
               inc="sv", waits=[e9])

    xln_evs = []
    dve_evs = []
    pool_evs = []
    for c in range(NC):
        g = dve_evs[c - 2] if c >= 2 else None
        eg = p.op("gpsimd", lambda g_, c=c: g_.tensor_sub(
            B["t1"][c % 2][:], B["rbuf"][:, c * T2:(c + 1) * T2], B["mb"][:]),
            inc="sg", waits=[e8, g])
        pool_evs.append(eg)
        evx = p.op("vector", lambda v, c=c: v.tensor_mul(
            B["xb"][:, c * T2:(c + 1) * T2], B["t1"][c % 2][:], B["rb"][:]),
            inc="sv", waits=[eg, e10])
        dve_evs.append(evx)
        xln_evs.append(evx)
        if xln_dma is not None:
            dst, qeng = xln_dma
            p.dma(qeng, dst[c], B["xb"][:, c * T2:(c + 1) * T2], "so",
                  waits=[evx])
    done = dict(pool=pool_evs[-1], dve=dve_evs[-1])
    return xln_evs, done


def ln_bufs(E, nc, pfx=""):
    return dict(
        rbuf=E(nc.sbuf_tensor(pfx + "rbuf", [128, NC * T2], F32)),
        xb=E(nc.sbuf_tensor(pfx + "xb", [128, NC * T2], BF16)),
        x2=[E(nc.sbuf_tensor(f"{pfx}x2_{i}", [128, T2], BF16)) for i in range(2)],
        t1=[E(nc.sbuf_tensor(f"{pfx}t1_{i}", [128, T2], F32)) for i in range(2)],
        ones1=E(nc.sbuf_tensor(pfx + "ones1", [128, 1], BF16)),
        onesf=E(nc.sbuf_tensor(pfx + "onesf", [1, 128], F32)),
        stat=E(nc.sbuf_tensor(pfx + "stat", [1, 4 * T2], F32)),
        epst=E(nc.sbuf_tensor(pfx + "epst", [1, 1], F32)),
        mb=E(nc.sbuf_tensor(pfx + "mb", [128, T2], F32)),
        rb=E(nc.sbuf_tensor(pfx + "rb", [128, T2], F32)),
        ps_s=E(nc.psum_tensor(pfx + "ps_s", [1, T2], F32)),
        ps_q=E(nc.psum_tensor(pfx + "ps_q", [1, T2], F32)),
        ps_b=E(nc.psum_tensor(pfx + "ps_b", [128, T2], F32)),
    )


def emit_ln_consts(p, B):
    p.op("vector", lambda v: v.memset(B["ones1"][:], 1.0), inc="sv")
    p.op("vector", lambda v: v.memset(B["onesf"][:], 1.0), inc="sv")
    p.op("vector", lambda v: v.memset(B["epst"][:], LN_EPS), inc="sv")
    B["cev"] = p.ev("sv")
    return B["cev"]




def build_L1():
    nc = bass.Bass("TRN2", target_bir_lowering=False, debug=False)
    r_in = nc.dram_tensor("r", [D, TCORE], F32, kind="ExternalInput").ap()
    a_in = nc.dram_tensor("a_t", [128, NC * RANK], F32, kind="ExternalInput").ap()
    w8_in = nc.dram_tensor("w8", [128, NC * NEXP], F32, kind="ExternalInput").ap()
    b8_in = nc.dram_tensor("b8", [NEXP, 1], F32, kind="ExternalInput").ap()
    lim_in = nc.dram_tensor("limes", [NEXP, RANK], F32, kind="ExternalInput").ap()
    id_in = nc.dram_tensor("ident", [128, 128], BF16, kind="ExternalInput").ap()

    xln_out = nc.dram_tensor("xln", [D, TCORE], BF16, kind="ExternalOutput").ap()
    z_out = nc.dram_tensor("z", [RANK, TCORE], F32, kind="ExternalOutput").ap()
    h8_out = nc.dram_tensor("h8", [NEXP, TCORE], F32, kind="ExternalOutput").ap()
    dsl_out = nc.dram_tensor("dsl", [NEXP, T2], F32, kind="ExternalOutput").ap()
    lmax_out = nc.dram_tensor("lmax", [1, 2], F32, kind="ExternalOutput").ap()

    with ExitStack() as stk:
        E = stk.enter_context
        p = Prog(nc, stk)
        B = ln_bufs(E, nc)
        a_f = E(nc.sbuf_tensor("a_f", [128, NC * RANK], F32))
        a_sb = E(nc.sbuf_tensor("a_sb", [128, NC * RANK], BF16))
        w8_f = E(nc.sbuf_tensor("w8_f", [128, NC * NEXP], F32))
        w8_sb = E(nc.sbuf_tensor("w8_sb", [128, NC * NEXP], BF16))
        b8_sb = E(nc.sbuf_tensor("b8_sb", [NEXP, 1], F32))
        lim_sb = E(nc.sbuf_tensor("lim_sb", [NEXP, RANK], F32))
        limb = E(nc.sbuf_tensor("limb", [NEXP, RANK], BF16))
        limt = E(nc.sbuf_tensor("limt", [RANK, NEXP], BF16))
        id_sb = E(nc.sbuf_tensor("id_sb", [128, 128], BF16))
        zbuf = E(nc.sbuf_tensor("zbuf", [RANK, TCORE], F32))
        h8buf = E(nc.sbuf_tensor("h8buf", [NEXP, TCORE], F32))
        zanc = E(nc.sbuf_tensor("zanc", [RANK, T2], BF16))
        dslb = E(nc.sbuf_tensor("dslb", [NEXP, T2], F32))
        mx8 = E(nc.sbuf_tensor("mx8", [NEXP, 2], F32))
        lmaxb = E(nc.sbuf_tensor("lmaxb", [1, 2], F32))
        ps_h8 = E(nc.psum_tensor("ps_h8", [NEXP, T2], F32))
        ps_z = E(nc.psum_tensor("ps_z", [RANK, T2], F32))
        ps_t = E(nc.psum_tensor("ps_t", [RANK, NEXP], BF16))
        ps_d = E(nc.psum_tensor("ps_d", [NEXP, T2], F32))

        p.dma("sync", a_f[:], a_in, "lc")
        p.dma("sync", w8_f[:], w8_in, "lc")
        p.dma("sync", b8_sb[:], b8_in, "lc")
        p.dma("sync", lim_sb[:], lim_in, "lc")
        p.dma("sync", id_sb[:], id_in, "lc")
        ea = ew = eb = el = ei = p.ev("lc")
        emit_ln_consts(p, B)
        eca = p.op("vector", lambda v: v.tensor_copy(a_sb[:], a_f[:]),
                   inc="sv", waits=[ea])
        ecw = p.op("vector", lambda v: v.tensor_copy(w8_sb[:], w8_f[:]),
                   inc="sv", waits=[ew])
        ecl = p.op("vector", lambda v: v.tensor_copy(limb[:], lim_sb[:]),
                   inc="sv", waits=[el])
        ett = p.op("tensor", lambda t: t.transpose(ps_t[:], limb[:],
                                                   id_sb[0:NEXP, 0:NEXP]),
                   inc="sp", waits=[ecl, ei])
        elt = p.op("vector", lambda v: v.tensor_copy(limt[:], ps_t[:]),
                   inc="sv", waits=[ett])

        prev = {}
        eh_evs, ez_evs = [], []
        for t2 in range(NT2):
            for c in range(NC):
                p.dma(
                    "sync", B["rbuf"][:, c * T2:(c + 1) * T2],
                    r_in[c * 128:(c + 1) * 128, t2 * T2:(t2 + 1) * T2], "ld",
                    waits=[prev.get("pool")] if c == 0 else ())
            r_ready = p.ev("ld")
            xdst = [xln_out[c * 128:(c + 1) * 128, t2 * T2:(t2 + 1) * T2]
                    for c in range(NC)]
            xln_evs, done = emit_layernorm(p, B, r_ready, prev, xln_dma=(xdst, "gpsimd"))
            e_mm = None
            for c in range(NC):
                w0 = [xln_evs[c], eca, ecw]
                if c == 0:
                    w0 += [eh_evs[-1] if eh_evs else None,
                           ez_evs[-1] if ez_evs else None]
                p.op("tensor", lambda t, c=c: t.matmul(
                    ps_h8[:], w8_sb[:, c * NEXP:(c + 1) * NEXP],
                    B["xb"][:, c * T2:(c + 1) * T2],
                    start=(c == 0), stop=(c == NC - 1)), waits=w0)
                e_mm = p.op("tensor", lambda t, c=c: t.matmul(
                    ps_z[:], a_sb[:, c * RANK:(c + 1) * RANK],
                    B["xb"][:, c * T2:(c + 1) * T2],
                    start=(c == 0), stop=(c == NC - 1)),
                    inc="sp" if c == NC - 1 else None)
            eh = p.op("vector", lambda v, t2=t2: v.tensor_scalar_add(
                h8buf[:, t2 * T2:(t2 + 1) * T2], ps_h8[:], b8_sb[:]),
                inc="sv", waits=[e_mm, eb])
            ez = p.op("vector", lambda v, t2=t2: v.tensor_copy(
                zbuf[:, t2 * T2:(t2 + 1) * T2], ps_z[:]), inc="sv")
            eh_evs.append(eh)
            ez_evs.append(ez)
            p.dma("gpsimd", h8_out[:, t2 * T2:(t2 + 1) * T2],
                  h8buf[:, t2 * T2:(t2 + 1) * T2], "so", waits=[eh])
            p.dma("gpsimd", z_out[:, t2 * T2:(t2 + 1) * T2],
                  zbuf[:, t2 * T2:(t2 + 1) * T2], "so", waits=[ez])
            prev = dict(done)
            prev["pe"] = e_mm
            prev["so"] = p.ev("so")

        eza = p.op("vector", lambda v: v.tensor_copy(zanc[:], zbuf[:, 1:TCORE:2]),
                   inc="sv", waits=[ez_evs[-1]])
        edm = p.op("tensor", lambda t: t.matmul(ps_d[:], limt[:], zanc[:],
                                                start=True, stop=True),
                   inc="sp", waits=[eza, elt])
        eds = p.op("vector", lambda v: v.tensor_copy(dslb[:], ps_d[:]),
                   inc="sv", waits=[edm])
        p.dma("gpsimd", dsl_out, dslb[:], "so", waits=[eds])
        em1 = p.op("vector", lambda v: v.reduce_max(
            mx8[:, 0:1], h8buf[:, 1:TCORE:2], axis=AX.X,
            apply_absolute_value=True), inc="sv", waits=[eh_evs[-1]])
        em2 = p.op("vector", lambda v: v.reduce_max(
            mx8[:, 1:2], dslb[:], axis=AX.X, apply_absolute_value=True),
            inc="sv", waits=[eds])
        eg = p.op("gpsimd", lambda g: g.tensor_reduce(
            out=lmaxb[:], in_=mx8[:], axis=AX.C, op=ALU.max),
            inc="sg", waits=[em1, em2])
        p.dma("gpsimd", lmax_out, lmaxb[:], "so", waits=[eg])
        p.wait("sync", p.ev("so"))
        p.emit()
    return nc




def tile4(w, kin, kout):
    return np.ascontiguousarray(
        w.reshape(kin, 128, kout, 128).transpose(2, 1, 0, 3)
        .reshape(kout, 128, kin * 128))


IDENT_BF16 = np.ascontiguousarray(np.eye(128, dtype=np.float32).astype(BF16NP))


def host_L1_weights(W, b, A, LiMEs):
    W_T = np.ascontiguousarray(W.T)
    w8 = np.ascontiguousarray(
        W_T[:, 0:NEXP].reshape(NC, 128, NEXP).transpose(1, 0, 2)
        .reshape(128, NC * NEXP))
    a_t = np.ascontiguousarray(
        A.reshape(NC, 128, RANK).transpose(1, 0, 2).reshape(128, NC * RANK))
    b8 = np.ascontiguousarray(b[0:NEXP].reshape(NEXP, 1))
    return dict(a_t=a_t, w8=w8, b8=b8,
                limes=np.ascontiguousarray(LiMEs), ident=IDENT_BF16)





def build_L2():
    nc = bass.Bass("TRN2", target_bir_lowering=False, debug=False)
    r_in = nc.dram_tensor("r", [D, TCORE], F32, kind="ExternalInput").ap()
    xln_in = nc.dram_tensor("xln", [D, TCORE], BF16, kind="ExternalInput").ap()
    z_in = nc.dram_tensor("z", [RANK, TCORE], F32, kind="ExternalInput").ap()
    h8_in = nc.dram_tensor("h8", [NEXP, TCORE], F32, kind="ExternalInput").ap()
    dsl_in = nc.dram_tensor("dsl", [NEXP, T2], F32, kind="ExternalInput").ap()
    hm_in = nc.dram_tensor("hmax8", [1, NCORES], F32, kind="ExternalInput").ap()
    dm_in = nc.dram_tensor("dmax8", [1, NCORES], F32, kind="ExternalInput").ap()
    w_in = nc.dram_tensor("w", [NC, 128, NC * 128], F32, kind="ExternalInput").ap()
    bm_in = nc.dram_tensor("bmat", [RANK, D], F32, kind="ExternalInput").ap()
    br_in = nc.dram_tensor("brow", [1, D], F32, kind="ExternalInput").ap()
    lim_in = nc.dram_tensor("limes", [NEXP, RANK], F32, kind="ExternalInput").ap()
    id_in = nc.dram_tensor("ident", [128, 128], BF16, kind="ExternalInput").ap()
    idf_in = nc.dram_tensor("identf", [NEXP, NEXP], F32, kind="ExternalInput").ap()
    rout = nc.dram_tensor("rout", [D, TCORE], F32, kind="ExternalOutput").ap()

    with ExitStack() as stk:
        E = stk.enter_context
        p = Prog(nc, stk)
        xln_sb = E(nc.sbuf_tensor("xln_sb", [128, NC * TCORE], BF16))
        z_sb = E(nc.sbuf_tensor("z_sb", [RANK, TCORE], F32))
        h8_sb = E(nc.sbuf_tensor("h8_sb", [NEXP, TCORE], F32))
        dsl_sb = E(nc.sbuf_tensor("dsl_sb", [NEXP, T2], F32))
        hm_sb = E(nc.sbuf_tensor("hm_sb", [1, NCORES], F32))
        dm_sb = E(nc.sbuf_tensor("dm_sb", [1, NCORES], F32))
        bm_f = E(nc.sbuf_tensor("bm_f", [RANK, D], F32))
        bmh = E(nc.sbuf_tensor("bmh", [RANK, D], BF16))
        br_f = E(nc.sbuf_tensor("br_f", [1, D], F32))
        br_b = E(nc.sbuf_tensor("br_b", [1, D], BF16))
        lim_sb = E(nc.sbuf_tensor("lim_sb", [NEXP, RANK], F32))
        limb = E(nc.sbuf_tensor("limb", [NEXP, RANK], BF16))
        id_sb = E(nc.sbuf_tensor("id_sb", [128, 128], BF16))
        idf_sb = E(nc.sbuf_tensor("idf_sb", [NEXP, NEXP], F32))
        ones512 = E(nc.sbuf_tensor("ones512", [1, T2], BF16))
        onesf8 = E(nc.sbuf_tensor("onesf8", [1, NEXP], F32))
        sc = E(nc.sbuf_tensor("sc", [1, 8], F32))
        ih8 = E(nc.sbuf_tensor("ih8", [NEXP, 2], F32))
        logit = E(nc.sbuf_tensor("logit", [NEXP, T2], F32))
        glt = E(nc.sbuf_tensor("glt", [128, 4 * NEXP], F32))
        gsc = E(nc.sbuf_tensor("gsc", [128, 8], F32))
        trep = E(nc.sbuf_tensor("trep", [128, NEXP], F32))
        mrep = E(nc.sbuf_tensor("mrep", [128, NEXP], F32))
        probs = E(nc.sbuf_tensor("probs", [128, NEXP], F32))
        sigm = E(nc.sbuf_tensor("sigm", [128, NEXP], F32))
        wgt = E(nc.sbuf_tensor("wgt", [128, NEXP], F32))
        wn = E(nc.sbuf_tensor("wn", [128, NEXP], BF16))
        wfm = E(nc.sbuf_tensor("wfm", [NEXP, T2], BF16))
        wf = E(nc.sbuf_tensor("wf", [NEXP, TCORE], BF16))
        zp = E(nc.sbuf_tensor("zp", [RANK, TCORE], BF16))
        wf32 = [E(nc.sbuf_tensor(f"wf32_{i}", [128, NC * 128], F32)) for i in range(2)]
        wbb = [E(nc.sbuf_tensor(f"wbb_{i}", [128, NC * 128], BF16)) for i in range(2)]
        rt = [E(nc.sbuf_tensor(f"rt_{i}", [128, TCORE], F32)) for i in range(2)]

        ps_sw = [[E(nc.psum_tensor(f"ps_sw_{i}_{j}", [128, T2], F32))
                  for j in range(2)] for i in range(2)]
        ps_tr = E(nc.psum_tensor("ps_tr", [128, NEXP], F32))
        ps_tb = E(nc.psum_tensor("ps_tb", [NEXP, 128], BF16))
        ps_pm = E(nc.psum_tensor("ps_pm", [RANK, T2], F32))
        ps_8 = E(nc.psum_tensor("ps_8", [NEXP, 2], F32))

        p.dma("sync", z_sb[:], z_in, "lc")
        p.dma("sync", h8_sb[:], h8_in, "lc")
        p.dma("sync", dsl_sb[:], dsl_in, "lc")
        p.dma("sync", hm_sb[:], hm_in, "lc")
        p.dma("sync", dm_sb[:], dm_in, "lc")
        p.dma("sync", bm_f[:], bm_in, "lc")
        p.dma("sync", br_f[:], br_in, "lc")
        p.dma("sync", lim_sb[:], lim_in, "lc")
        p.dma("sync", id_sb[:], id_in, "lc")
        p.dma("sync", idf_sb[:], idf_in, "lc")
        for c in range(NC):
            p.dma("sync", xln_sb[:, c * TCORE:(c + 1) * TCORE],
                  xln_in[c * 128:(c + 1) * 128, :], "lc")
        elc = p.ev("lc")

        ec0 = p.op("vector", lambda v: v.memset(ones512[:], 1.0), inc="sv")
        p.op("vector", lambda v: v.memset(onesf8[:], 1.0), inc="sv")
        ecb = p.op("vector", lambda v: v.tensor_scalar_mul(bmh[:], bm_f[:], 0.5),
                   inc="sv", waits=[elc])
        ecr = p.op("vector", lambda v: v.tensor_copy(br_b[:], br_f[:]),
                   inc="sv", waits=[elc])
        ecl = p.op("vector", lambda v: v.tensor_copy(limb[:], lim_sb[:]),
                   inc="sv", waits=[elc])

        e1 = p.op("vector", lambda v: v.reduce_max(sc[:, 0:1], hm_sb[:],
                                                   axis=AX.X),
                  inc="sv", waits=[elc])
        e2 = p.op("vector", lambda v: v.reduce_max(sc[:, 1:2], dm_sb[:],
                                                   axis=AX.X), inc="sv")
        e3 = p.op("vector", lambda v: v.tensor_scalar_max(sc[:, 0:2], sc[:, 0:2],
                                                          G_EPS),
                  inc="sv", waits=[e1, e2])
        e4 = p.op("vector", lambda v: v.reciprocal(sc[:, 2:4], sc[:, 0:2]),
                  inc="sv", waits=[e3])
        e5 = p.op("tensor", lambda t: t.matmul(ps_8[:], onesf8[:], sc[:, 2:4],
                                               start=True, stop=True),
                  inc="sp", waits=[e4, ec0])
        e6 = p.op("vector", lambda v: v.tensor_scalar_mul(ih8[:], ps_8[:], 0.5),
                  inc="sv", waits=[e5])
        e7 = p.op("vector", lambda v: v.tensor_scalar(
            out=logit[:], in0=h8_sb[:, 1:TCORE:2], scalar1=ih8[:, 0:1],
            scalar2=None, op0=ALU.mult), inc="sv", waits=[e6, elc])
        e8 = p.op("vector", lambda v: v.scalar_tensor_tensor(
            out=logit[:], in0=dsl_sb[:], scalar=ih8[:, 1:2], in1=logit[:],
            op0=ALU.mult, op1=ALU.add), inc="sv", waits=[e7])

        wn_evs = []
        for j in range(4):
            jt = p.op("tensor", lambda t, j=j: t.transpose(
                ps_tr[:], logit[:, j * 128:(j + 1) * 128],
                idf_sb[:]), inc="sp",
                waits=[e8, elc, wn_evs[-1] if wn_evs else None])
            g0 = p.op("vector", lambda v, j=j: v.tensor_copy(
                glt[:, j * NEXP:(j + 1) * NEXP], ps_tr[:]), inc="sv",
                waits=[jt])
            pr = glt[:, j * NEXP:(j + 1) * NEXP]
            g1 = p.op("vector", lambda v, pr=pr: v.reduce_max(
                gsc[:, 0:1], pr, axis=AX.X, negate=True), inc="sv", waits=[g0])
            g2 = p.op("scalar", lambda s, pr=pr: s.activation(
                out=probs[:], in_=pr, func=AF.Exp, bias=gsc[:, 0:1],
                accum_out=gsc[:, 1:2]), inc="sa", waits=[g1])
            g3 = p.op("vector", lambda v: v.reciprocal(gsc[:, 2:3], gsc[:, 1:2]),
                      inc="sv", waits=[g2])
            g4 = p.op("vector", lambda v: v.tensor_scalar_mul(
                probs[:], probs[:], gsc[:, 2:3]), inc="sv", waits=[g3])
            g5 = p.op("vector", lambda v: v.memset(trep[:], 1e30), inc="sv",
                      waits=[g4])
            g6 = p.op("vector", lambda v: v.reduce_max(trep[:, 0:1], probs[:],
                                                       axis=AX.X),
                      inc="sv", waits=[g5])
            g7 = p.op("vector", lambda v: v.match_replace(
                out=mrep[:], in_to_replace=trep[:], in_values=probs[:],
                imm_value=-1e30), inc="sv", waits=[g6])
            g8 = p.op("vector", lambda v: v.reduce_max(gsc[:, 3:4], mrep[:],
                                                       axis=AX.X),
                      inc="sv", waits=[g7])
            g9 = p.op("vector", lambda v: v.tensor_scalar_mul(
                gsc[:, 4:5], gsc[:, 3:4], -2.0), inc="sv", waits=[g8])
            g10 = p.op("scalar", lambda s: s.activation(
                out=sigm[:], in_=probs[:], func=AF.Sigmoid, scale=2.0,
                bias=gsc[:, 4:5]), inc="sa", waits=[g9])
            g11 = p.op("vector", lambda v: v.tensor_mul(wgt[:], probs[:],
                                                        sigm[:]),
                       inc="sv", waits=[g10])
            g12 = p.op("vector", lambda v: v.reduce_sum(gsc[:, 5:6], wgt[:],
                                                        axis=AX.X),
                       inc="sv", waits=[g11])
            g13 = p.op("vector", lambda v: v.tensor_scalar_add(
                gsc[:, 6:7], gsc[:, 5:6], 1e-9), inc="sv", waits=[g12])
            g14 = p.op("vector", lambda v: v.reciprocal(gsc[:, 7:8],
                                                        gsc[:, 6:7]),
                       inc="sv", waits=[g13])
            g15 = p.op("vector", lambda v: v.tensor_scalar_mul(
                wn[:], wgt[:], gsc[:, 7:8]), inc="sv", waits=[g14])
            g16 = p.op("tensor", lambda t: t.transpose(ps_tb[:], wn[:],
                                                       id_sb[:]),
                       inc="sp", waits=[g15])
            g17 = p.op("vector", lambda v, j=j: v.tensor_copy(
                wfm[:, j * 128:(j + 1) * 128], ps_tb[:]), inc="sv",
                waits=[g16])
            wn_evs.append(g17)
        ewf0 = p.op("vector", lambda v: v.tensor_copy(wf[:, 0:TCORE:2], wfm[:]),
                    inc="sv", waits=[wn_evs[-1]])
        ewf1 = p.op("vector", lambda v: v.tensor_copy(wf[:, 1:TCORE:2], wfm[:]),
                    inc="sv")
        zp_evs = []
        for t2 in range(NT2):
            epm = p.op("tensor", lambda t, t2=t2: t.matmul(
                ps_pm[:], limb[:], wf[:, t2 * T2:(t2 + 1) * T2],
                start=True, stop=True), inc="sp",
                waits=[ewf1, ecl, zp_evs[-1] if zp_evs else None])
            ezp = p.op("vector", lambda v, t2=t2: v.tensor_mul(
                zp[:, t2 * T2:(t2 + 1) * T2], z_sb[:, t2 * T2:(t2 + 1) * T2],
                ps_pm[:]), inc="sv", waits=[epm])
            zp_evs.append(ezp)

        cast_evs = []
        mm_evs = []
        ev_evs = []
        st_evs = []

        def wdma(o):
            b = o % 2
            gate = cast_evs[o - 2] if o >= 2 else None
            return p.dma("sync", wf32[b][:], w_in[o], f"wl{b}", waits=[gate])

        def rdma(o):
            b = o % 2
            gate = st_evs[o - 2] if o >= 2 else None
            return p.dma("sync", rt[b][:], r_in[o * 128:(o + 1) * 128, :],
                         f"rl{b}", waits=[gate])

        wd = [wdma(0), wdma(1)]
        rd = [rdma(0), rdma(1)]
        for o in range(NC):
            b = o % 2
            gate = mm_evs[o - 2] if o >= 2 else None
            ec = p.op("vector", lambda v, b=b: v.tensor_copy(wbb[b][:],
                                                             wf32[b][:]),
                      inc="sv", waits=[wd[o], gate, elc])
            cast_evs.append(ec)
            pgate = ev_evs[2 * (o - 2) + 1] if o >= 2 else None
            for k in range(NC):
                w0 = [ec] if k == 0 else []
                if k == 0 and pgate is not None:
                    w0.append(pgate)
                if k == 0:
                    w0 += [zp_evs[-1], ecb, ecr]
                p.op("tensor", lambda t, b=b, k=k: t.matmul(
                    ps_sw[b][0][:], wbb[b][:, k * 128:(k + 1) * 128],
                    xln_sb[:, (2 * k) * T2:(2 * k + 1) * T2],
                    start=(k == 0), stop=False), waits=w0)
                p.op("tensor", lambda t, b=b, k=k: t.matmul(
                    ps_sw[b][1][:], wbb[b][:, k * 128:(k + 1) * 128],
                    xln_sb[:, (2 * k + 1) * T2:(2 * k + 2) * T2],
                    start=(k == 0), stop=False))
            for j in range(2):
                p.op("tensor", lambda t, b=b, j=j, o=o: t.matmul(
                    ps_sw[b][j][:], bmh[:, o * 128:(o + 1) * 128],
                    zp[:, j * T2:(j + 1) * T2], start=False, stop=False))
            p.op("tensor", lambda t, b=b, o=o: t.matmul(
                ps_sw[b][0][:], br_b[:, o * 128:(o + 1) * 128], ones512[:],
                start=False, stop=True))
            em = p.op("tensor", lambda t, b=b, o=o: t.matmul(
                ps_sw[b][1][:], br_b[:, o * 128:(o + 1) * 128], ones512[:],
                start=False, stop=True), inc="sp")
            mm_evs.append(em)
            for j in range(2):
                eev = p.op("vector", lambda v, b=b, j=j: v.scalar_tensor_tensor(
                    out=rt[b][:, j * T2:(j + 1) * T2], in0=ps_sw[b][j][:],
                    scalar=0.5, in1=rt[b][:, j * T2:(j + 1) * T2],
                    op0=ALU.mult, op1=ALU.add),
                    inc="sv", waits=[em, rd[o]])
                ev_evs.append(eev)
            st_evs.append(
                p.dma("gpsimd", rout[o * 128:(o + 1) * 128, :], rt[b][:],
                      f"sb{b}", waits=[ev_evs[-1]]))
            if o + 2 < NC:
                wd.append(wdma(o + 2))
                rd.append(rdma(o + 2))
        p.wait("sync", p.ev("sb0"))
        p.wait("sync", p.ev("sb1"))
        p.emit()
    return nc


def host_L2_weights(W, b, Bmat, LiMEs):
    W_T = np.ascontiguousarray(W.T)
    return dict(w=tile4(W_T, NC, NC), bmat=np.ascontiguousarray(Bmat),
                brow=np.ascontiguousarray(b.reshape(1, D)),
                limes=np.ascontiguousarray(LiMEs), ident=IDENT_BF16,
                identf=np.eye(NEXP, dtype=np.float32))





def build_L3():
    nc = bass.Bass("TRN2", target_bir_lowering=False, debug=False)
    r_in = nc.dram_tensor("r", [D, TCORE], F32, kind="ExternalInput").ap()
    qc_in = nc.dram_tensor("qc", [NC, 128, NC * 128], I32, kind="ExternalInput").ap()
    kc_in = nc.dram_tensor("kc", [NKVH, 128, NC * 128], I32, kind="ExternalInput").ap()
    vc_in = nc.dram_tensor("vc", [NKVH, 128, NC * 128], I32, kind="ExternalInput").ap()
    qs_in = nc.dram_tensor("qs", [RANK, D], F32, kind="ExternalInput").ap()
    ks_in = nc.dram_tensor("ks", [RANK, DKV], F32, kind="ExternalInput").ap()
    vs_in = nc.dram_tensor("vs", [RANK, DKV], F32, kind="ExternalInput").ap()
    qp_in = nc.dram_tensor("qpos", [1, TCORE], F32, kind="ExternalInput").ap()
    if_in = nc.dram_tensor("invf", [1, 64], F32, kind="ExternalInput").ap()
    rq_in = nc.dram_tensor("redq", [64, TCORE], F32, kind="ExternalInput").ap()
    rc_in = nc.dram_tensor("redc", [64, TCORE], F32, kind="ExternalInput").ap()
    pr_in = nc.dram_tensor("prot", [128, 128], BF16, kind="ExternalInput").ap()
    id_in = nc.dram_tensor("ident", [128, 128], BF16, kind="ExternalInput").ap()
    q_out = nc.dram_tensor("q", [D, TCORE], BF16, kind="ExternalOutput").ap()
    k_out = nc.dram_tensor("ko", [DKV, TCORE], BF16, kind="ExternalOutput").ap()
    v_out = nc.dram_tensor("vo", [TCORE, DKV], BF16, kind="ExternalOutput").ap()
    qstage = nc.dram_tensor("qstage", [NC, RANK * 128], BF16).ap()
    kstage = nc.dram_tensor("kstage", [NKVH, RANK * 128], BF16).ap()
    vstage = nc.dram_tensor("vstage", [NKVH, RANK * 128], BF16).ap()
    wqc = nc.dram_tensor("wqc", [NC, 128, NC * 128], BF16).ap()
    wkc = nc.dram_tensor("wkc", [NKVH, 128, NC * 128], BF16).ap()
    wvc = nc.dram_tensor("wvc", [NKVH, 128, NC * 128], BF16).ap()

    with ExitStack() as stk:
        E = stk.enter_context
        p = Prog(nc, stk)
        B = ln_bufs(E, nc)
        scf = [E(nc.sbuf_tensor(f"scf{i}", [RANK, 128], F32)) for i in range(2)]
        scb = [E(nc.sbuf_tensor(f"scb{i}", [RANK, 128], BF16)) for i in range(2)]
        qp_sb = E(nc.sbuf_tensor("qp_sb", [1, TCORE], F32))
        invf = E(nc.sbuf_tensor("invf_sb", [1, 64], F32))
        rq_sb = E(nc.sbuf_tensor("rq_sb", [64, TCORE], F32))
        rc_sb = E(nc.sbuf_tensor("rc_sb", [64, TCORE], F32))
        pr_sb = E(nc.sbuf_tensor("pr_sb", [128, 128], BF16))
        id_sb = E(nc.sbuf_tensor("id_sb", [128, 128], BF16))
        cod = [E(nc.sbuf_tensor(f"cod{i}", [128, NC * 128], I32)) for i in range(2)]
        sbc = [E(nc.sbuf_tensor(f"sbc{i}", [128, NC * 128], BF16)) for i in range(2)]
        wdq = [E(nc.sbuf_tensor(f"wdq{i}", [128, NC * 128], BF16)) for i in range(2)]
        trg = E(nc.sbuf_tensor("trg", [64, 2 * T2], F32))
        cos2 = E(nc.sbuf_tensor("cos2", [128, T2], BF16))
        sin2 = E(nc.sbuf_tensor("sin2", [128, T2], BF16))
        raw = [E(nc.sbuf_tensor(f"raw{i}", [128, T2], BF16)) for i in range(2)]
        tmp = [E(nc.sbuf_tensor(f"tmp{i}", [128, T2], BF16)) for i in range(2)]
        fin = [E(nc.sbuf_tensor(f"fin{i}", [128, T2], BF16)) for i in range(2)]
        vtm = [E(nc.sbuf_tensor(f"vtm{i}", [128, 128], BF16)) for i in range(2)]
        ps_p = [E(nc.psum_tensor(f"ps_p{i}", [128, T2], F32)) for i in range(2)]
        ps_rot = E(nc.psum_tensor("ps_rot", [128, T2], F32))
        ps_tt = E(nc.psum_tensor("ps_tt", [128, 128], BF16))

        p.dma("sync", qp_sb[:], qp_in, "lc")
        p.dma("sync", invf[:], if_in, "lc")
        p.dma("sync", rq_sb[:], rq_in, "lc")
        p.dma("sync", rc_sb[:], rc_in, "lc")
        p.dma("sync", pr_sb[:], pr_in, "lc")
        p.dma("sync", id_sb[:], id_in, "lc")
        elc = p.ev("lc")
        emit_ln_consts(p, B)
        ei = elc
        sc_cast = [None, None]
        sc_st = [None, None]
        jobs = ([(qs_in, qstage, o) for o in range(NC)]
                + [(ks_in, kstage, o) for o in range(NKVH)]
                + [(vs_in, vstage, o) for o in range(NKVH)])
        for n, (src_t, dst_t, o) in enumerate(jobs):
            par = n % 2
            eld = p.dma("sync", scf[par][:], src_t[:, o * 128:(o + 1) * 128],
                        f"sl{par}", waits=[sc_cast[par]])
            ecst = p.op("vector", lambda v, par=par: v.tensor_scalar_mul(
                scb[par][:], scf[par][:], 0.5), inc="sv",
                waits=[eld, sc_st[par]])
            sc_cast[par] = ecst
            sc_st[par] = p.dma(
                "gpsimd",
                dst_t[o:o + 1, :].rearrange("o (a b) -> (o a) b", a=RANK),
                scb[par][:], f"stg{par}", waits=[ecst])
        estg0, estg1 = p.ev("stg0"), p.ev("stg1")

        deq_ev = [None, None]
        mm_ev = [None, None]
        rawrd_ev = [None, None]
        finst_ev = [None, None]
        vst_ev = [None, None]
        last_fin = [None]
        last_vtc = [None]

        wst_ev = [None, None]

        def proj_chunk(codes_src, stage_src, par, cache_dst=None,
                       cache_src=None):
            gate = mm_ev[par]
            if cache_src is not None:
                ed = p.dma("sync", wdq[par][:], cache_src, f"c{par}",
                           waits=[gate, wst_ev[par]])
                deq_ev[par] = ed
                return ed
            ec = p.dma("sync", cod[par][:], codes_src, f"c{par}", waits=[gate])
            ebc = p.dma(
                "sync", sbc[par][:],
                stage_src.broadcast_to([128, NC * 128]), f"b{par}",
                waits=[estg0, estg1])
            ed = p.op("vector", lambda v, par=par: v.scalar_tensor_tensor(
                out=wdq[par][:], in0=cod[par][:], scalar=8.0, in1=sbc[par][:],
                op0=ALU.subtract, op1=ALU.mult), inc="sv",
                waits=[ec, ebc, gate, wst_ev[par]])
            deq_ev[par] = ed
            if cache_dst is not None:
                wst_ev[par] = p.dma("gpsimd", cache_dst, wdq[par][:],
                                    f"wc{par}", waits=[ed])
            return ed

        def mms(par, extra):
            ev = None
            for k in range(NC):
                w = ([deq_ev[par], rawrd_ev[par]] + extra) if k == 0 else []
                ev = p.op("tensor", lambda t, par=par, k=k: t.matmul(
                    ps_p[par][:], wdq[par][:, k * 128:(k + 1) * 128],
                    B["xb"][:, k * T2:(k + 1) * T2],
                    start=(k == 0), stop=(k == NC - 1)),
                    inc="sp" if k == NC - 1 else None, waits=w)
            mm_ev[par] = ev
            return ev

        def rope_store(par, dst, etab):
            e_raw = p.op("vector", lambda v, par=par: v.tensor_copy(
                raw[par][:], ps_p[par][:]), inc="sv",
                waits=[mm_ev[par], rawrd_ev[par]])
            e_rot = p.op("tensor", lambda t, par=par: t.matmul(
                ps_rot[:], pr_sb[:], raw[par][:], start=True, stop=True),
                inc="sp", waits=[e_raw, elc, last_fin[0]])
            e_tt = p.op("vector", lambda v, par=par: v.tensor_mul(
                tmp[par][:], raw[par][:], cos2[:]), inc="sv",
                waits=[etab, e_raw])
            e_fin = p.op("vector", lambda v, par=par: v.scalar_tensor_tensor(
                out=fin[par][:], in0=ps_rot[:], scalar=1.0, in1=sin2[:],
                op0=ALU.mult, op1=ALU.mult), inc="sv",
                waits=[e_rot, etab, finst_ev[par]])
            e_fin2 = p.op("vector", lambda v, par=par: v.tensor_add(
                fin[par][:], fin[par][:], tmp[par][:]), inc="sv",
                waits=[e_tt, e_fin])
            last_fin[0] = e_fin
            rawrd_ev[par] = e_tt
            est = p.dma("gpsimd", dst, fin[par][:], f"s{par}", waits=[e_fin2])
            finst_ev[par] = est
            return est

        prev = {}
        for t2 in range(NT2):
            for c in range(NC):
                p.dma("sync", B["rbuf"][:, c * T2:(c + 1) * T2],
                      r_in[c * 128:(c + 1) * 128, t2 * T2:(t2 + 1) * T2], "ld",
                      waits=[prev.get("pool")] if c == 0 else ())
            r_ready = p.ev("ld")
            xln_evs, done = emit_layernorm(p, B, r_ready, prev)
            xln_all = xln_evs[-1]

            ang = B["ps_b"][0:64, :]
            s_in = trg[:, 0:T2]
            c_in = trg[:, T2:2 * T2]
            tsl = slice(t2 * T2, (t2 + 1) * T2)
            er0 = p.op("tensor", lambda t, t2=t2: t.matmul(
                ang, invf[:], qp_sb[:, t2 * T2:(t2 + 1) * T2],
                start=True, stop=True), inc="sp", waits=[ei, xln_all])
            er4 = p.op("vector", lambda v, tsl=tsl: v.scalar_tensor_tensor(
                out=s_in, in0=rq_sb[:, tsl], scalar=-2 * PI, in1=ang,
                op0=ALU.mult, op1=ALU.add), inc="sv",
                waits=[er0, prev.get("rope")])
            er5 = p.op("vector", lambda v: v.tensor_scalar(
                out=s_in, in0=s_in, scalar1=PI, scalar2=-PI,
                op0=ALU.min, op1=ALU.max), inc="sv", waits=[er4])
            ec4 = p.op("vector", lambda v, tsl=tsl: v.scalar_tensor_tensor(
                out=c_in, in0=rc_sb[:, tsl], scalar=-2 * PI, in1=ang,
                op0=ALU.mult, op1=ALU.add), inc="sv", waits=[er0])
            ec5 = p.op("vector", lambda v: v.tensor_scalar(
                out=c_in, in0=c_in, scalar1=PI / 2, scalar2=None,
                op0=ALU.add), inc="sv", waits=[ec4])
            ec6 = p.op("vector", lambda v: v.tensor_scalar(
                out=c_in, in0=c_in, scalar1=PI, scalar2=-PI,
                op0=ALU.min, op1=ALU.max), inc="sv", waits=[ec5])
            eb1 = p.op("scalar", lambda s: s.activation(
                out=sin2[0:64, :], in_=s_in, func=AF.Sin, scale=-1.0),
                inc="sa", waits=[er5, prev.get("rope")])
            eb2 = p.op("scalar", lambda s: s.activation(
                out=sin2[64:128, :], in_=s_in, func=AF.Sin), inc="sa")
            eb3 = p.op("scalar", lambda s: s.activation(
                out=cos2[0:64, :], in_=c_in, func=AF.Sin), inc="sa",
                waits=[ec6])
            eb4 = p.op("scalar", lambda s: s.activation(
                out=cos2[64:128, :], in_=c_in, func=AF.Sin), inc="sa")
            etab = eb4

            for o in range(NKVH):
                par = o % 2
                proj_chunk(kc_in[o], kstage[o:o + 1, :], par,
                           cache_dst=wkc[o] if t2 == 0 else None,
                           cache_src=wkc[o] if t2 == 1 else None)
                mms(par, [xln_all])
                rope_store(par, k_out[o * 128:(o + 1) * 128,
                                      t2 * T2:(t2 + 1) * T2], etab)
            for o in range(NKVH):
                par = o % 2
                proj_chunk(vc_in[o], vstage[o:o + 1, :], par,
                           cache_dst=wvc[o] if t2 == 0 else None,
                           cache_src=wvc[o] if t2 == 1 else None)
                mms(par, [xln_all])
                e_raw = p.op("vector", lambda v, par=par: v.tensor_copy(
                    raw[par][:], ps_p[par][:]), inc="sv",
                    waits=[mm_ev[par], rawrd_ev[par]])
                etr_last = None
                for i in range(4):
                    etr = p.op("tensor", lambda t, par=par, i=i: t.transpose(
                        ps_tt[:], raw[par][:, i * 128:(i + 1) * 128],
                        id_sb[:]), inc="sp",
                        waits=[e_raw, elc, last_vtc[0]])
                    vpar = i % 2
                    evc = p.op("vector", lambda v, vpar=vpar: v.tensor_copy(
                        vtm[vpar][:], ps_tt[:]), inc="sv",
                        waits=[etr, vst_ev[vpar]])
                    last_vtc[0] = evc
                    est = p.dma("gpsimd",
                                v_out[t2 * T2 + i * 128:t2 * T2 + (i + 1) * 128,
                                      o * 128:(o + 1) * 128],
                                vtm[vpar][:], f"v{vpar}", waits=[evc])
                    vst_ev[vpar] = est
                    etr_last = etr
                rawrd_ev[par] = etr_last
            for o in range(NC):
                par = o % 2
                proj_chunk(qc_in[o], qstage[o:o + 1, :], par)
                mms(par, [xln_all])
                rope_store(par, q_out[o * 128:(o + 1) * 128,
                                      t2 * T2:(t2 + 1) * T2], etab)

            prev = dict(done)
            prev["pe"] = p.ev("sp")
            prev["dve"] = p.ev("sv")
            prev["rope"] = p.ev("sv")
        for s in ("s0", "s1", "v0", "v1"):
            p.wait("sync", p.ev(s))
        p.emit()
    return nc


def host_rope(tok):
    invf = (1.0 / THETA ** (np.arange(64) / 64.0)).astype(np.float32)
    pos = tok.astype(np.float32)
    ang = invf[:, None] * pos[None, :]
    redq = np.round(ang / np.float32(2 * PI)).astype(np.float32)
    redc = np.round((ang + np.float32(PI / 2)) / np.float32(2 * PI)).astype(np.float32)
    return dict(invf=invf.reshape(1, 64), redq=np.ascontiguousarray(redq),
                redc=np.ascontiguousarray(redc))


def host_L3_weights(q_codes, k_codes, v_codes, q_scales, k_scales, v_scales):
    return dict(
        qc=tile4(q_codes, NC, NC), kc=tile4(k_codes, NC, NKVH),
        vc=tile4(v_codes, NC, NKVH),
        qs=np.ascontiguousarray(q_scales), ks=np.ascontiguousarray(k_scales),
        vs=np.ascontiguousarray(v_scales),
        prot=np.ascontiguousarray(
            np.roll(np.eye(128, dtype=np.float32), 64, axis=1).astype(BF16NP)),
        ident=IDENT_BF16)





def build_L4():
    nc = bass.Bass("TRN2", target_bir_lowering=False, debug=False)
    r_in = nc.dram_tensor("r", [D, TCORE], F32, kind="ExternalInput").ap()
    q_in = nc.dram_tensor("q", [D, TCORE], BF16, kind="ExternalInput").ap()
    kg_in = nc.dram_tensor("kg", [DKV, SEQ], BF16, kind="ExternalInput").ap()
    vg_in = nc.dram_tensor("vg", [NKVH, SEQ, DH], BF16, kind="ExternalInput").ap()
    oc_in = nc.dram_tensor("oc", [NC, 128, NC * 128], I32, kind="ExternalInput").ap()
    os_in = nc.dram_tensor("osc", [RANK, D], F32, kind="ExternalInput").ap()
    qp_in = nc.dram_tensor("qpos", [1, TCORE], F32, kind="ExternalInput").ap()
    ia_in = nc.dram_tensor("iap", [128, 1], F32, kind="ExternalInput").ap()
    rout = nc.dram_tensor("rout", [D, TCORE], F32, kind="ExternalOutput").ap()
    ostage = nc.dram_tensor("ostage", [NC, RANK * 128], BF16).ap()

    TQ = 256
    NJ = 4

    with ExitStack() as stk:
        E = stk.enter_context
        p = Prog(nc, stk)
        qsb = E(nc.sbuf_tensor("qsb", [128, NC * TQ], BF16))
        att = E(nc.sbuf_tensor("att", [128, NC * TCORE], BF16))
        ksb = E(nc.sbuf_tensor("ksb", [128, 16 * 128], BF16))
        vsb = E(nc.sbuf_tensor("vsb", [128, 16 * 128], BF16))
        prb = [E(nc.sbuf_tensor(f"prb{i}", [128, 4 * TQ], BF16)) for i in range(2)]
        msk = E(nc.sbuf_tensor("msk", [128, 4 * TQ], BF16))
        mskf = E(nc.sbuf_tensor("mskf", [128, TQ], F32))
        qpb = E(nc.sbuf_tensor("qpb", [128, TQ], F32))
        ia_sb = E(nc.sbuf_tensor("ia_sb", [128, 1], F32))
        ones1 = E(nc.sbuf_tensor("ones1", [128, 1], BF16))
        onesf = E(nc.sbuf_tensor("onesf", [1, 128], F32))
        srow = E(nc.sbuf_tensor("srow", [1, TQ], F32))
        sbf = [E(nc.sbuf_tensor(f"sbf{i}", [128, TQ], F32)) for i in range(2)]
        scf = [E(nc.sbuf_tensor(f"scf{i}", [RANK, 128], F32)) for i in range(2)]
        scb = [E(nc.sbuf_tensor(f"scb{i}", [RANK, 128], BF16)) for i in range(2)]
        cod = [E(nc.sbuf_tensor(f"cod{i}", [128, NC * 128], I32)) for i in range(2)]
        sbc = [E(nc.sbuf_tensor(f"sbc{i}", [128, NC * 128], BF16)) for i in range(2)]
        wdq = [E(nc.sbuf_tensor(f"wdq{i}", [128, NC * 128], BF16)) for i in range(2)]
        rt = [E(nc.sbuf_tensor(f"rt_{i}", [128, TCORE], F32)) for i in range(2)]

        ps_sc = [E(nc.psum_tensor(f"ps_sc{i}", [128, 2 * TQ], F32))
                 for i in range(2)]
        ps_at = [E(nc.psum_tensor(f"ps_at{i}", [128, TQ], F32)) for i in range(2)]
        ps_S = E(nc.psum_tensor("ps_S", [1, TQ], F32))
        ps_bc = E(nc.psum_tensor("ps_bc", [128, TQ], F32))
        ps_o = [E(nc.psum_tensor(f"ps_o{i}", [128, T2], F32)) for i in range(2)]

        p.dma("sync", ia_sb[:], ia_in, "lc")
        elc = p.ev("lc")
        ec1 = p.op("vector", lambda v: v.memset(ones1[:], 1.0), inc="sv")
        ec2 = p.op("vector", lambda v: v.memset(onesf[:], 1.0), inc="sv")
        ecst = [None, None]
        est_ = [None, None]
        for n in range(NC):
            par = n % 2
            eld = p.dma("sync", scf[par][:], os_in[:, n * 128:(n + 1) * 128],
                        f"sl{par}", waits=[ecst[par]])
            ee = p.op("vector", lambda v, par=par: v.tensor_scalar_mul(
                scb[par][:], scf[par][:], 0.5), inc="sv",
                waits=[eld, est_[par]])
            ecst[par] = ee
            est_[par] = p.dma(
                "gpsimd",
                ostage[n:n + 1, :].rearrange("o (a b) -> (o a) b", a=RANK),
                scb[par][:], f"stg{par}", waits=[ee])
        estg0, estg1 = p.ev("stg0"), p.ev("stg1")

        att_ev = []
        kld_prev = None
        vld_prev = None
        for j in range(NJ):
            nch = 4 * j + 4
            qgate = att_ev[-1] if att_ev else None
            for c in range(NC):
                p.dma("sync", qsb[:, c * TQ:(c + 1) * TQ],
                      q_in[c * 128:(c + 1) * 128, j * TQ:(j + 1) * TQ], "lq",
                      waits=[qgate] if c == 0 else ())
            eq = p.ev("lq")
            ebq = p.dma("sync", qpb[:],
                        qp_in[0:1, j * TQ:(j + 1) * TQ].broadcast_to([128, TQ]),
                        "lqb", waits=[att_ev[-1] if att_ev else None])
            mk_evs = []
            for mi in range(4):
                ci = nch - 4 + mi
                e1_ = p.op("vector", lambda v, ci=ci: v.tensor_scalar(
                    out=mskf[:], in0=qpb[:], scalar1=ia_sb[:, 0:1],
                    scalar2=float(128 * ci) - 0.5, op0=ALU.subtract,
                    op1=ALU.subtract), inc="sv",
                    waits=[ebq, elc, mk_evs[-1] if mi else att_ev[-1] if att_ev else None])
                e2_ = p.op("scalar", lambda s: s.activation(
                    out=mskf[:], in_=mskf[:], func=AF.Sign), inc="sa",
                    waits=[e1_])
                e3_ = p.op("vector", lambda v, mi=mi: v.tensor_scalar(
                    out=msk[:, mi * TQ:(mi + 1) * TQ], in0=mskf[:], scalar1=0.5,
                    scalar2=0.5, op0=ALU.mult, op1=ALU.add), inc="sv",
                    waits=[e2_])
                mk_evs.append(e3_)
            emsk = mk_evs[-1]

            for h in range(NKVH):
                ek = p.dma("sync", ksb[:, 0:nch * 128],
                           kg_in[h * 128:(h + 1) * 128, 0:nch * 128], "lk",
                           waits=[kld_prev])
                ev_ = []
                for ci in range(nch):
                    ev_.append(p.dma(
                        "sync", vsb[:, ci * 128:(ci + 1) * 128],
                        vg_in[h, ci * 128:(ci + 1) * 128, :], "lv",
                        waits=[vld_prev] if ci == 0 else ()))
                ekv = (p.ev("lk"), p.ev("lv"))
                for hq in range(4 * h, 4 * h + 4):
                    qh = hq
                    ngr = (nch + 1) // 2
                    exp_evs = []
                    spv_evs = []
                    apar = qh % 2
                    aslot = j % 2
                    for g in range(ngr):
                        c0 = 2 * g
                        c1 = min(nch, c0 + 2)
                        par = g % 2
                        for ci in range(c0, c1):
                            w = []
                            if ci == 0:
                                w = [ekv[0], ekv[1], eq,
                                     att_ev[-1] if att_ev else None]
                            if ci == c0:
                                w.append(exp_evs[-2] if len(exp_evs) >= 2
                                         else (att_ev[-1] if att_ev else None))
                            p.op("tensor", lambda t, ci=ci, qh=qh, par=par:
                                 t.matmul(
                                     ps_sc[par][:, (ci % 2) * TQ:
                                                (ci % 2 + 1) * TQ],
                                     ksb[:, ci * 128:(ci + 1) * 128],
                                     qsb[:, qh * TQ:(qh + 1) * TQ],
                                     start=True, stop=True),
                                 inc="sp" if ci == c1 - 1 else None, waits=w)
                        eqk = p.ev("sp")
                        wex = [eqk]
                        if len(spv_evs) >= 2:
                            wex.append(spv_evs[-2])
                        elif not spv_evs and att_ev:
                            wex.append(att_ev[-1])
                        ee = p.op("scalar", lambda s, c0=c0, c1=c1, par=par:
                                  s.activation(
                                      out=prb[par][:, 0:(c1 - c0) * TQ],
                                      in_=ps_sc[par][:, 0:(c1 - c0) * TQ],
                                      func=AF.Exp,
                                      scale=float(1.0 / np.sqrt(DH))),
                                  inc="sa", waits=wex)
                        exp_evs.append(ee)
                        emg = ee
                        for ci in range(max(c0, nch - 4), c1):
                            mi = ci - (nch - 4)
                            emg = p.op("vector", lambda v, ci=ci, par=par, mi=mi:
                                       v.tensor_mul(
                                           prb[par][:, (ci % 2) * TQ:
                                                    (ci % 2 + 1) * TQ],
                                           prb[par][:, (ci % 2) * TQ:
                                                    (ci % 2 + 1) * TQ],
                                           msk[:, mi * TQ:(mi + 1) * TQ]),
                                       inc="sv", waits=[emg, emsk])
                        ela = None
                        for ci in range(c0, c1):
                            w = [emg] if ci == c0 else []
                            if ci == 0:
                                w.append(("sv", p.cnt["sv"]))
                            p.op("tensor", lambda t, ci=ci, par=par, nch=nch:
                                 t.matmul(
                                     ps_S[:], ones1[:],
                                     prb[par][:, (ci % 2) * TQ:
                                              (ci % 2 + 1) * TQ],
                                     start=(ci == 0), stop=(ci == nch - 1)),
                                 inc="sp" if ci == nch - 1 else None,
                                 waits=w)
                            ela = p.op("tensor", lambda t, ci=ci, par=par,
                                       apar=apar, nch=nch: t.matmul(
                                           ps_at[apar][:],
                                           vsb[:, ci * 128:(ci + 1) * 128],
                                           prb[par][:, (ci % 2) * TQ:
                                                    (ci % 2 + 1) * TQ],
                                           start=(ci == 0),
                                           stop=(ci == nch - 1)),
                                       inc="sp" if ci == nch - 1 else None)
                        spv_evs.append(ela if ela is not None else p.ev("sp"))
                    epv = p.ev("sp")
                    es1 = p.op("vector", lambda v: v.tensor_copy(
                        srow[:], ps_S[:]), inc="sv", waits=[epv])
                    es2 = p.op("vector", lambda v: v.reciprocal(
                        srow[:], srow[:]), inc="sv", waits=[es1])
                    es3 = p.op("tensor", lambda t: t.matmul(
                        ps_bc[:], onesf[:], srow[:],
                        start=True, stop=True), inc="sp", waits=[es2, ec2])
                    es4 = p.op("vector", lambda v, apar=apar: v.tensor_copy(
                        sbf[apar][:], ps_bc[:]), inc="sv", waits=[es3])
                    es5 = p.op("vector", lambda v, qh=qh, j=j, apar=apar:
                               v.scalar_tensor_tensor(
                                   out=att[:, qh * TCORE + j * TQ:
                                           qh * TCORE + (j + 1) * TQ],
                                   in0=ps_at[apar][:], scalar=1.0,
                                   in1=sbf[apar][:], op0=ALU.mult,
                                   op1=ALU.mult), inc="sv", waits=[es4])
                    att_ev.append(es5)
                kld_prev = p.ev("sv")
                vld_prev = p.ev("sv")

        deq_ev = [None, None]
        mm_ev = [None, None]
        st_evs = []
        rd = []

        def wdma(o):
            par = o % 2
            gate = deq_ev[par]
            ec = p.dma("sync", cod[par][:], oc_in[o], f"c{par}", waits=[gate])
            ebc = p.dma("sync", sbc[par][:],
                        ostage[o:o + 1, :].broadcast_to([128, NC * 128]),
                        f"b{par}", waits=[estg0, estg1])
            return (ec, ebc)

        def rdma(o):
            b = o % 2
            gate = st_evs[o - 2] if o >= 2 else None
            return p.dma("sync", rt[b][:], r_in[o * 128:(o + 1) * 128, :],
                         f"rl{b}", waits=[gate])

        pre = [wdma(0), wdma(1)]
        rd = [rdma(0), rdma(1)]
        ev_evs = []
        for o in range(NC):
            par = o % 2
            ec, ebc = pre[o]
            gate = mm_ev[par]
            ed = p.op("vector", lambda v, par=par: v.scalar_tensor_tensor(
                out=wdq[par][:], in0=cod[par][:], scalar=8.0, in1=sbc[par][:],
                op0=ALU.subtract, op1=ALU.mult), inc="sv",
                waits=[ec, ebc, gate])
            deq_ev[par] = ed
            em = None
            for k in range(NC):
                w = [ed, att_ev[-1]] if k == 0 else []
                if k == 0 and o >= 1:
                    w.append(ev_evs[2 * (o - 1) + 1])
                for th in range(2):
                    em = p.op("tensor", lambda t, par=par, k=k, th=th: t.matmul(
                        ps_o[th][:], wdq[par][:, k * 128:(k + 1) * 128],
                        att[:, k * TCORE + th * T2:k * TCORE + (th + 1) * T2],
                        start=(k == 0), stop=(k == NC - 1)),
                        inc="sp" if (k == NC - 1 and th == 1) else None,
                        waits=w if th == 0 else ())
            mm_ev[par] = em
            for th in range(2):
                eev = p.op("vector", lambda v, par=par, th=th:
                           v.scalar_tensor_tensor(
                               out=rt[par][:, th * T2:(th + 1) * T2],
                               in0=ps_o[th][:], scalar=1.0,
                               in1=rt[par][:, th * T2:(th + 1) * T2],
                               op0=ALU.mult, op1=ALU.add),
                           inc="sv", waits=[em, rd[o]])
                ev_evs.append(eev)
            st_evs.append(
                p.dma("gpsimd", rout[o * 128:(o + 1) * 128, :], rt[par][:],
                      f"sb{par}", waits=[ev_evs[-1]]))
            if o + 2 < NC:
                pre.append(wdma(o + 2))
                rd.append(rdma(o + 2))
        p.wait("sync", p.ev("sb0"))
        p.wait("sync", p.ev("sb1"))
        p.emit()
    return nc


def host_L4_fixed(o_codes, o_scales):
    return dict(oc=tile4(o_codes, NC, NC), osc=np.ascontiguousarray(o_scales),
                iap=np.arange(128, dtype=np.float32).reshape(128, 1))


def host_kv_glue(ko_pair, vo_pair):
    kg = np.empty((DKV, SEQ), dtype=BF16NP)
    vg = np.empty((NKVH, SEQ, DH), dtype=BF16NP)
    for g in range(16):
        pcore = (g // 2) % 2
        slot = 2 * (g // 4) + (g % 2)
        kg[:, g * 128:(g + 1) * 128] = ko_pair[pcore][:, slot * 128:(slot + 1) * 128]
        for h in range(NKVH):
            vg[h, g * 128:(g + 1) * 128, :] = \
                vo_pair[pcore][slot * 128:(slot + 1) * 128, h * 128:(h + 1) * 128]
    return kg, vg


def kernel(L0_attn_x, L0_attn_q_codes, L0_attn_k_codes, L0_attn_v_codes,
           L0_attn_o_codes, L0_attn_q_scales, L0_attn_k_scales,
           L0_attn_v_scales, L0_attn_o_scales, L0_ffn_W, L0_ffn_b, L0_ffn_A,
           L0_ffn_Bmat, L0_ffn_LiMEs):
    x = np.asarray(L0_attn_x)
    cores = list(range(NCORES))
    toks = [core_token_idx(c) for c in cores]

    nc1 = build_L1()
    nc2 = build_L2()
    nc3 = build_L3()
    nc4 = build_L4()

    w1 = host_L1_weights(L0_ffn_W, L0_ffn_b, L0_ffn_A, L0_ffn_LiMEs)
    w2 = host_L2_weights(L0_ffn_W, L0_ffn_b, L0_ffn_Bmat, L0_ffn_LiMEs)
    w3 = host_L3_weights(L0_attn_q_codes, L0_attn_k_codes, L0_attn_v_codes,
                         L0_attn_q_scales, L0_attn_k_scales, L0_attn_v_scales)
    w4 = host_L4_fixed(L0_attn_o_codes, L0_attn_o_scales)
    rope = {0: host_rope(toks[0]), 1: host_rope(toks[1])}
    qpos = {par: toks[par].astype(np.float32).reshape(1, TCORE)
            for par in (0, 1)}

    def ffn_launches(rs):
        m1 = []
        for c in cores:
            m = dict(w1)
            m["r"] = rs[c]
            m1.append(m)
        r1 = run_bass_kernel_spmd(nc1, m1, core_ids=cores)
        hmax8 = np.ascontiguousarray(np.array(
            [[r1.results[c]["lmax"][0, 0] for c in cores]], np.float32))
        dmax8 = np.ascontiguousarray(np.array(
            [[r1.results[c]["lmax"][0, 1] for c in cores]], np.float32))
        m2 = []
        for c in cores:
            o = r1.results[c]
            m = dict(w2)
            m.update(r=rs[c], xln=o["xln"], z=o["z"], h8=o["h8"], dsl=o["dsl"],
                     hmax8=hmax8, dmax8=dmax8)
            m2.append(m)
        r2 = run_bass_kernel_spmd(nc2, m2, core_ids=cores)
        return [r2.results[c]["rout"] for c in cores]

    rs = [np.ascontiguousarray(x[c // 2, toks[c]].T.astype(np.float32))
          for c in cores]
    rs = ffn_launches(rs)
    m3 = []
    for c in cores:
        m = dict(w3)
        m.update(r=rs[c], qpos=qpos[c % 2])
        m.update(rope[c % 2])
        m3.append(m)
    r3 = run_bass_kernel_spmd(nc3, m3, core_ids=cores)
    m4 = []
    for c in cores:
        b = c // 2
        kg, vg = host_kv_glue(
            [r3.results[2 * b]["ko"], r3.results[2 * b + 1]["ko"]],
            [r3.results[2 * b]["vo"], r3.results[2 * b + 1]["vo"]])
        m = dict(w4)
        m.update(r=rs[c], q=r3.results[c]["q"], kg=kg, vg=vg,
                 qpos=qpos[c % 2])
        m4.append(m)
    r4 = run_bass_kernel_spmd(nc4, m4, core_ids=cores)
    rs = [r4.results[c]["rout"] for c in cores]
    rs = ffn_launches(rs)

    out = np.empty((x.shape[0], SEQ, D), np.float32)
    for c in cores:
        out[c // 2, toks[c]] = rs[c].T
    return out


# version 03 (final)
# speedup vs baseline: 1.2932x; 1.2932x over previous
import numpy as np
import ml_dtypes
from contextlib import ExitStack

import concourse.bass as bass
import concourse.mybir as mybir
from concourse.bass_utils import run_bass_kernel_spmd

F32 = mybir.dt.float32
BF16 = mybir.dt.bfloat16
I32 = mybir.dt.int32
AF = mybir.ActivationFunctionType
ALU = mybir.AluOpType
AX = mybir.AxisListType
BF16NP = ml_dtypes.bfloat16

NCORES = 8
D = 4096
NC = 32
TCORE = 1024
T2 = 512
NT2 = 2
NH = 32
NKVH = 8
DH = 128
DKV = NKVH * DH
RANK = 32
NEXP = 8
LN_EPS = 1e-5
G_EPS = 1e-6
THETA = 10000.0
PI = float(np.pi)
SEQ = 2048

EVEN_TILES = [0, 1, 4, 5, 8, 9, 12, 13]
ODD_TILES = [2, 3, 6, 7, 10, 11, 14, 15]


def core_tiles(c):
    return EVEN_TILES if c % 2 == 0 else ODD_TILES


def core_token_idx(c):
    return np.concatenate([np.arange(t * 128, (t + 1) * 128) for t in core_tiles(c)])



ENGS = ("sync", "scalar", "vector", "tensor", "gpsimd")


class Prog:

    def __init__(self, nc, stack):
        self.nc = nc
        self.stack = stack
        self.ops = {e: [] for e in ENGS}
        self.sems = {}
        self.cnt = {}
        self.last_wait = {}

    def sem(self, name):
        if name not in self.sems:
            self.sems[name] = self.stack.enter_context(self.nc.semaphore(name))
            self.cnt[name] = 0
        return self.sems[name]

    def wait(self, eng, ev):
        if ev is None:
            return
        s, v = ev
        if v <= 0:
            return
        key = (eng, s)
        if self.last_wait.get(key, 0) >= v:
            return
        self.last_wait[key] = v
        h = self.sem(s)
        self.ops[eng].append(lambda e, h=h, v=v: e.wait_ge(h, v))

    def op(self, eng, fn, inc=None, waits=(), inc_by=1):
        for ev in waits:
            self.wait(eng, ev)
        ev = None
        if inc is not None:
            h = self.sem(inc)
            self.cnt[inc] += inc_by
            ev = (inc, self.cnt[inc])
            self.ops[eng].append(
                lambda e, fn=fn, h=h, k=inc_by: fn(e).then_inc(h, k))
        else:
            self.ops[eng].append(fn)
        return ev

    def dma(self, eng, out, in_, inc, waits=()):
        return self.op(eng, lambda e: e.dma_start(out=out, in_=in_),
                       inc=inc, waits=waits, inc_by=16)

    def ev(self, semname):
        return (semname, self.cnt.get(semname, 0))

    def emit(self):
        blk = self.stack.enter_context(self.nc.Block())
        for eng in ENGS:
            lst = self.ops[eng]
            if not lst:
                continue

            def body(e, lst=lst):
                for f in lst:
                    f(e)

            getattr(blk, eng)(body)




def emit_layernorm(p, B, r_ready_ev, prev, xln_dma=None):
    casts = []
    sq_evs = []
    for c in range(NC):
        ev = p.op("scalar", lambda s, c=c: s.activation(
            out=B["xb"][:, c * T2:(c + 1) * T2],
            in_=B["rbuf"][:, c * T2:(c + 1) * T2], func=AF.Copy),
            inc="sa", waits=[r_ready_ev, prev.get("pe"), prev.get("so")])
        g = sq_evs[c - 2][1] if c >= 2 else prev.get("pe")
        ev2 = p.op("scalar", lambda s, c=c: s.activation(
            out=B["x2"][c % 2][:], in_=B["rbuf"][:, c * T2:(c + 1) * T2],
            func=AF.Square), inc="sa", waits=[g])
        casts.append((ev, ev2))
        w0 = [ev2] + ([prev.get("dve"), B.get("cev")] if c == 0 else [])
        p.op("tensor", lambda t, c=c: t.matmul(
            B["ps_s"][:], B["ones1"][:], B["xb"][:, c * T2:(c + 1) * T2],
            start=(c == 0), stop=(c == NC - 1)), waits=w0)
        e_mm = p.op("tensor", lambda t, c=c: t.matmul(
            B["ps_q"][:], B["ones1"][:], B["x2"][c % 2][:],
            start=(c == 0), stop=(c == NC - 1)), inc="sp")
        sq_evs.append((None, e_mm))
    stats_done = p.ev("sp")

    st = B["stat"]
    m_, msq, sd, rstd = (st[:, i * T2:(i + 1) * T2] for i in range(4))
    var = msq
    e1 = p.op("vector", lambda v: v.tensor_scalar_mul(m_, B["ps_s"][:], 1.0 / D),
              inc="sv", waits=[stats_done])
    e2 = p.op("vector", lambda v: v.tensor_scalar_mul(msq, B["ps_q"][:], 1.0 / D),
              inc="sv", waits=[e1])
    e3 = p.op("vector", lambda v: v.tensor_mul(sd, m_, m_), inc="sv", waits=[e2])
    e4 = p.op("vector", lambda v: v.tensor_sub(var, msq, sd), inc="sv", waits=[e3])
    e5 = p.op("scalar", lambda s: s.activation(out=sd, in_=var, func=AF.Sqrt,
                                               bias=B["epst"][:]),
              inc="sa", waits=[e4])
    e6 = p.op("vector", lambda v: v.reciprocal(rstd, sd), inc="sv", waits=[e5])
    e7 = p.op("tensor", lambda t: t.matmul(B["ps_b"][:], B["onesf"][:], m_,
                                           start=True, stop=True),
              inc="sp", waits=[e6, prev.get("dve"), B.get("cev")])
    e8 = p.op("vector", lambda v: v.tensor_copy(B["mb"][:], B["ps_b"][:]),
              inc="sv", waits=[e7, prev.get("pool"), prev.get("dve")])
    e9 = p.op("tensor", lambda t: t.matmul(B["ps_b"][:], B["onesf"][:], rstd,
                                           start=True, stop=True),
              inc="sp", waits=[e8])
    e10 = p.op("vector", lambda v: v.tensor_copy(B["rb"][:], B["ps_b"][:]),
               inc="sv", waits=[e9])

    xln_evs = []
    dve_evs = []
    pool_evs = []
    for c in range(NC):
        g = dve_evs[c - 2] if c >= 2 else None
        eg = p.op("gpsimd", lambda g_, c=c: g_.tensor_sub(
            B["t1"][c % 2][:], B["rbuf"][:, c * T2:(c + 1) * T2], B["mb"][:]),
            inc="sg", waits=[e8, g])
        pool_evs.append(eg)
        evx = p.op("vector", lambda v, c=c: v.tensor_mul(
            B["xb"][:, c * T2:(c + 1) * T2], B["t1"][c % 2][:], B["rb"][:]),
            inc="sv", waits=[eg, e10])
        dve_evs.append(evx)
        xln_evs.append(evx)
        if xln_dma is not None:
            dst, qeng = xln_dma
            p.dma(qeng, dst[c], B["xb"][:, c * T2:(c + 1) * T2], "so",
                  waits=[evx])
    done = dict(pool=pool_evs[-1], dve=dve_evs[-1])
    return xln_evs, done


def ln_bufs(E, nc, pfx=""):
    return dict(
        rbuf=E(nc.sbuf_tensor(pfx + "rbuf", [128, NC * T2], F32)),
        xb=E(nc.sbuf_tensor(pfx + "xb", [128, NC * T2], BF16)),
        x2=[E(nc.sbuf_tensor(f"{pfx}x2_{i}", [128, T2], BF16)) for i in range(2)],
        t1=[E(nc.sbuf_tensor(f"{pfx}t1_{i}", [128, T2], F32)) for i in range(2)],
        ones1=E(nc.sbuf_tensor(pfx + "ones1", [128, 1], BF16)),
        onesf=E(nc.sbuf_tensor(pfx + "onesf", [1, 128], F32)),
        stat=E(nc.sbuf_tensor(pfx + "stat", [1, 4 * T2], F32)),
        epst=E(nc.sbuf_tensor(pfx + "epst", [1, 1], F32)),
        mb=E(nc.sbuf_tensor(pfx + "mb", [128, T2], F32)),
        rb=E(nc.sbuf_tensor(pfx + "rb", [128, T2], F32)),
        ps_s=E(nc.psum_tensor(pfx + "ps_s", [1, T2], F32)),
        ps_q=E(nc.psum_tensor(pfx + "ps_q", [1, T2], F32)),
        ps_b=E(nc.psum_tensor(pfx + "ps_b", [128, T2], F32)),
    )


def emit_ln_consts(p, B):
    p.op("vector", lambda v: v.memset(B["ones1"][:], 1.0), inc="sv")
    p.op("vector", lambda v: v.memset(B["onesf"][:], 1.0), inc="sv")
    p.op("vector", lambda v: v.memset(B["epst"][:], LN_EPS), inc="sv")
    B["cev"] = p.ev("sv")
    return B["cev"]




def build_L1():
    nc = bass.Bass("TRN2", target_bir_lowering=False, debug=False)
    r_in = nc.dram_tensor("r", [D, TCORE], F32, kind="ExternalInput").ap()
    a_in = nc.dram_tensor("a_t", [128, NC * RANK], F32, kind="ExternalInput").ap()
    w8_in = nc.dram_tensor("w8", [128, NC * NEXP], F32, kind="ExternalInput").ap()
    b8_in = nc.dram_tensor("b8", [NEXP, 1], F32, kind="ExternalInput").ap()
    lim_in = nc.dram_tensor("limes", [NEXP, RANK], F32, kind="ExternalInput").ap()
    id_in = nc.dram_tensor("ident", [128, 128], BF16, kind="ExternalInput").ap()

    xln_out = nc.dram_tensor("xln", [D, TCORE], BF16, kind="ExternalOutput").ap()
    z_out = nc.dram_tensor("z", [RANK, TCORE], F32, kind="ExternalOutput").ap()
    h8_out = nc.dram_tensor("h8", [NEXP, TCORE], F32, kind="ExternalOutput").ap()
    dsl_out = nc.dram_tensor("dsl", [NEXP, T2], F32, kind="ExternalOutput").ap()
    lmax_out = nc.dram_tensor("lmax", [1, 2], F32, kind="ExternalOutput").ap()

    with ExitStack() as stk:
        E = stk.enter_context
        p = Prog(nc, stk)
        B = ln_bufs(E, nc)
        a_f = E(nc.sbuf_tensor("a_f", [128, NC * RANK], F32))
        a_sb = E(nc.sbuf_tensor("a_sb", [128, NC * RANK], BF16))
        w8_f = E(nc.sbuf_tensor("w8_f", [128, NC * NEXP], F32))
        w8_sb = E(nc.sbuf_tensor("w8_sb", [128, NC * NEXP], BF16))
        b8_sb = E(nc.sbuf_tensor("b8_sb", [NEXP, 1], F32))
        lim_sb = E(nc.sbuf_tensor("lim_sb", [NEXP, RANK], F32))
        limb = E(nc.sbuf_tensor("limb", [NEXP, RANK], BF16))
        limt = E(nc.sbuf_tensor("limt", [RANK, NEXP], BF16))
        id_sb = E(nc.sbuf_tensor("id_sb", [128, 128], BF16))
        zbuf = E(nc.sbuf_tensor("zbuf", [RANK, TCORE], F32))
        h8buf = E(nc.sbuf_tensor("h8buf", [NEXP, TCORE], F32))
        zanc = E(nc.sbuf_tensor("zanc", [RANK, T2], BF16))
        dslb = E(nc.sbuf_tensor("dslb", [NEXP, T2], F32))
        mx8 = E(nc.sbuf_tensor("mx8", [NEXP, 2], F32))
        lmaxb = E(nc.sbuf_tensor("lmaxb", [1, 2], F32))
        ps_h8 = E(nc.psum_tensor("ps_h8", [NEXP, T2], F32))
        ps_z = E(nc.psum_tensor("ps_z", [RANK, T2], F32))
        ps_t = E(nc.psum_tensor("ps_t", [RANK, NEXP], BF16))
        ps_d = E(nc.psum_tensor("ps_d", [NEXP, T2], F32))

        p.dma("sync", a_f[:], a_in, "lc")
        p.dma("sync", w8_f[:], w8_in, "lc")
        p.dma("sync", b8_sb[:], b8_in, "lc")
        p.dma("sync", lim_sb[:], lim_in, "lc")
        p.dma("sync", id_sb[:], id_in, "lc")
        ea = ew = eb = el = ei = p.ev("lc")
        emit_ln_consts(p, B)
        eca = p.op("vector", lambda v: v.tensor_copy(a_sb[:], a_f[:]),
                   inc="sv", waits=[ea])
        ecw = p.op("vector", lambda v: v.tensor_copy(w8_sb[:], w8_f[:]),
                   inc="sv", waits=[ew])
        ecl = p.op("vector", lambda v: v.tensor_copy(limb[:], lim_sb[:]),
                   inc="sv", waits=[el])
        ett = p.op("tensor", lambda t: t.transpose(ps_t[:], limb[:],
                                                   id_sb[0:NEXP, 0:NEXP]),
                   inc="sp", waits=[ecl, ei])
        elt = p.op("vector", lambda v: v.tensor_copy(limt[:], ps_t[:]),
                   inc="sv", waits=[ett])

        prev = {}
        eh_evs, ez_evs = [], []
        for t2 in range(NT2):
            for c in range(NC):
                p.dma(
                    "sync", B["rbuf"][:, c * T2:(c + 1) * T2],
                    r_in[c * 128:(c + 1) * 128, t2 * T2:(t2 + 1) * T2], "ld",
                    waits=[prev.get("pool")] if c == 0 else ())
            r_ready = p.ev("ld")
            xdst = [xln_out[c * 128:(c + 1) * 128, t2 * T2:(t2 + 1) * T2]
                    for c in range(NC)]
            xln_evs, done = emit_layernorm(p, B, r_ready, prev, xln_dma=(xdst, "gpsimd"))
            e_mm = None
            for c in range(NC):
                w0 = [xln_evs[c], eca, ecw]
                if c == 0:
                    w0 += [eh_evs[-1] if eh_evs else None,
                           ez_evs[-1] if ez_evs else None]
                p.op("tensor", lambda t, c=c: t.matmul(
                    ps_h8[:], w8_sb[:, c * NEXP:(c + 1) * NEXP],
                    B["xb"][:, c * T2:(c + 1) * T2],
                    start=(c == 0), stop=(c == NC - 1)), waits=w0)
                e_mm = p.op("tensor", lambda t, c=c: t.matmul(
                    ps_z[:], a_sb[:, c * RANK:(c + 1) * RANK],
                    B["xb"][:, c * T2:(c + 1) * T2],
                    start=(c == 0), stop=(c == NC - 1)),
                    inc="sp" if c == NC - 1 else None)
            eh = p.op("vector", lambda v, t2=t2: v.tensor_scalar_add(
                h8buf[:, t2 * T2:(t2 + 1) * T2], ps_h8[:], b8_sb[:]),
                inc="sv", waits=[e_mm, eb])
            ez = p.op("vector", lambda v, t2=t2: v.tensor_copy(
                zbuf[:, t2 * T2:(t2 + 1) * T2], ps_z[:]), inc="sv")
            eh_evs.append(eh)
            ez_evs.append(ez)
            p.dma("gpsimd", h8_out[:, t2 * T2:(t2 + 1) * T2],
                  h8buf[:, t2 * T2:(t2 + 1) * T2], "so", waits=[eh])
            p.dma("gpsimd", z_out[:, t2 * T2:(t2 + 1) * T2],
                  zbuf[:, t2 * T2:(t2 + 1) * T2], "so", waits=[ez])
            prev = dict(done)
            prev["pe"] = e_mm
            prev["so"] = p.ev("so")

        eza = p.op("vector", lambda v: v.tensor_copy(zanc[:], zbuf[:, 1:TCORE:2]),
                   inc="sv", waits=[ez_evs[-1]])
        edm = p.op("tensor", lambda t: t.matmul(ps_d[:], limt[:], zanc[:],
                                                start=True, stop=True),
                   inc="sp", waits=[eza, elt])
        eds = p.op("vector", lambda v: v.tensor_copy(dslb[:], ps_d[:]),
                   inc="sv", waits=[edm])
        p.dma("gpsimd", dsl_out, dslb[:], "so", waits=[eds])
        em1 = p.op("vector", lambda v: v.reduce_max(
            mx8[:, 0:1], h8buf[:, 1:TCORE:2], axis=AX.X,
            apply_absolute_value=True), inc="sv", waits=[eh_evs[-1]])
        em2 = p.op("vector", lambda v: v.reduce_max(
            mx8[:, 1:2], dslb[:], axis=AX.X, apply_absolute_value=True),
            inc="sv", waits=[eds])
        eg = p.op("gpsimd", lambda g: g.tensor_reduce(
            out=lmaxb[:], in_=mx8[:], axis=AX.C, op=ALU.max),
            inc="sg", waits=[em1, em2])
        p.dma("gpsimd", lmax_out, lmaxb[:], "so", waits=[eg])
        p.wait("sync", p.ev("so"))
        p.emit()
    return nc




def tile4(w, kin, kout):
    return np.ascontiguousarray(
        w.reshape(kin, 128, kout, 128).transpose(2, 1, 0, 3)
        .reshape(kout, 128, kin * 128))


IDENT_BF16 = np.ascontiguousarray(np.eye(128, dtype=np.float32).astype(BF16NP))


def host_L1_weights(W, b, A, LiMEs):
    W_T = np.ascontiguousarray(W.T)
    w8 = np.ascontiguousarray(
        W_T[:, 0:NEXP].reshape(NC, 128, NEXP).transpose(1, 0, 2)
        .reshape(128, NC * NEXP))
    a_t = np.ascontiguousarray(
        A.reshape(NC, 128, RANK).transpose(1, 0, 2).reshape(128, NC * RANK))
    b8 = np.ascontiguousarray(b[0:NEXP].reshape(NEXP, 1))
    return dict(a_t=a_t, w8=w8, b8=b8,
                limes=np.ascontiguousarray(LiMEs), ident=IDENT_BF16)





def build_L2():
    nc = bass.Bass("TRN2", target_bir_lowering=False, debug=False)
    r_in = nc.dram_tensor("r", [D, TCORE], F32, kind="ExternalInput").ap()
    xln_in = nc.dram_tensor("xln", [D, TCORE], BF16, kind="ExternalInput").ap()
    z_in = nc.dram_tensor("z", [RANK, TCORE], F32, kind="ExternalInput").ap()
    h8_in = nc.dram_tensor("h8", [NEXP, TCORE], F32, kind="ExternalInput").ap()
    dsl_in = nc.dram_tensor("dsl", [NEXP, T2], F32, kind="ExternalInput").ap()
    hm_in = nc.dram_tensor("hmax8", [1, NCORES], F32, kind="ExternalInput").ap()
    dm_in = nc.dram_tensor("dmax8", [1, NCORES], F32, kind="ExternalInput").ap()
    w_in = nc.dram_tensor("w", [NC, 128, NC * 128], F32, kind="ExternalInput").ap()
    bm_in = nc.dram_tensor("bmat", [RANK, D], F32, kind="ExternalInput").ap()
    br_in = nc.dram_tensor("brow", [1, D], F32, kind="ExternalInput").ap()
    lim_in = nc.dram_tensor("limes", [NEXP, RANK], F32, kind="ExternalInput").ap()
    id_in = nc.dram_tensor("ident", [128, 128], BF16, kind="ExternalInput").ap()
    idf_in = nc.dram_tensor("identf", [NEXP, NEXP], F32, kind="ExternalInput").ap()
    rout = nc.dram_tensor("rout", [D, TCORE], F32, kind="ExternalOutput").ap()

    with ExitStack() as stk:
        E = stk.enter_context
        p = Prog(nc, stk)
        xln_sb = E(nc.sbuf_tensor("xln_sb", [128, NC * TCORE], BF16))
        z_sb = E(nc.sbuf_tensor("z_sb", [RANK, TCORE], F32))
        h8_sb = E(nc.sbuf_tensor("h8_sb", [NEXP, TCORE], F32))
        dsl_sb = E(nc.sbuf_tensor("dsl_sb", [NEXP, T2], F32))
        hm_sb = E(nc.sbuf_tensor("hm_sb", [1, NCORES], F32))
        dm_sb = E(nc.sbuf_tensor("dm_sb", [1, NCORES], F32))
        bm_f = E(nc.sbuf_tensor("bm_f", [RANK, D], F32))
        bmh = E(nc.sbuf_tensor("bmh", [RANK, D], BF16))
        br_f = E(nc.sbuf_tensor("br_f", [1, D], F32))
        br_b = E(nc.sbuf_tensor("br_b", [1, D], BF16))
        lim_sb = E(nc.sbuf_tensor("lim_sb", [NEXP, RANK], F32))
        limb = E(nc.sbuf_tensor("limb", [NEXP, RANK], BF16))
        id_sb = E(nc.sbuf_tensor("id_sb", [128, 128], BF16))
        idf_sb = E(nc.sbuf_tensor("idf_sb", [NEXP, NEXP], F32))
        ones512 = E(nc.sbuf_tensor("ones512", [1, T2], BF16))
        onesf8 = E(nc.sbuf_tensor("onesf8", [1, NEXP], F32))
        sc = E(nc.sbuf_tensor("sc", [1, 8], F32))
        ih8 = E(nc.sbuf_tensor("ih8", [NEXP, 2], F32))
        logit = E(nc.sbuf_tensor("logit", [NEXP, T2], F32))
        glt = E(nc.sbuf_tensor("glt", [128, 4 * NEXP], F32))
        gsc = E(nc.sbuf_tensor("gsc", [128, 8], F32))
        trep = E(nc.sbuf_tensor("trep", [128, NEXP], F32))
        mrep = E(nc.sbuf_tensor("mrep", [128, NEXP], F32))
        probs = E(nc.sbuf_tensor("probs", [128, NEXP], F32))
        sigm = E(nc.sbuf_tensor("sigm", [128, NEXP], F32))
        wgt = E(nc.sbuf_tensor("wgt", [128, NEXP], F32))
        wn = E(nc.sbuf_tensor("wn", [128, NEXP], BF16))
        wfm = E(nc.sbuf_tensor("wfm", [NEXP, T2], BF16))
        wf = E(nc.sbuf_tensor("wf", [NEXP, TCORE], BF16))
        zp = E(nc.sbuf_tensor("zp", [RANK, TCORE], BF16))
        wf32 = [E(nc.sbuf_tensor(f"wf32_{i}", [128, NC * 128], F32)) for i in range(2)]
        wbb = [E(nc.sbuf_tensor(f"wbb_{i}", [128, NC * 128], BF16)) for i in range(2)]
        rt = [E(nc.sbuf_tensor(f"rt_{i}", [128, TCORE], F32)) for i in range(2)]

        ps_sw = [[E(nc.psum_tensor(f"ps_sw_{i}_{j}", [128, T2], F32))
                  for j in range(2)] for i in range(2)]
        ps_tr = E(nc.psum_tensor("ps_tr", [128, NEXP], F32))
        ps_tb = E(nc.psum_tensor("ps_tb", [NEXP, 128], BF16))
        ps_pm = E(nc.psum_tensor("ps_pm", [RANK, T2], F32))
        ps_8 = E(nc.psum_tensor("ps_8", [NEXP, 2], F32))

        p.dma("sync", z_sb[:], z_in, "lc")
        p.dma("sync", h8_sb[:], h8_in, "lc")
        p.dma("sync", dsl_sb[:], dsl_in, "lc")
        p.dma("sync", hm_sb[:], hm_in, "lc")
        p.dma("sync", dm_sb[:], dm_in, "lc")
        p.dma("sync", bm_f[:], bm_in, "lc")
        p.dma("sync", br_f[:], br_in, "lc")
        p.dma("sync", lim_sb[:], lim_in, "lc")
        p.dma("sync", id_sb[:], id_in, "lc")
        p.dma("sync", idf_sb[:], idf_in, "lc")
        for c in range(NC):
            p.dma("sync", xln_sb[:, c * TCORE:(c + 1) * TCORE],
                  xln_in[c * 128:(c + 1) * 128, :], "lc")
        elc = p.ev("lc")

        ec0 = p.op("vector", lambda v: v.memset(ones512[:], 1.0), inc="sv")
        p.op("vector", lambda v: v.memset(onesf8[:], 1.0), inc="sv")
        ecb = p.op("vector", lambda v: v.tensor_scalar_mul(bmh[:], bm_f[:], 0.5),
                   inc="sv", waits=[elc])
        ecr = p.op("vector", lambda v: v.tensor_copy(br_b[:], br_f[:]),
                   inc="sv", waits=[elc])
        ecl = p.op("vector", lambda v: v.tensor_copy(limb[:], lim_sb[:]),
                   inc="sv", waits=[elc])

        e1 = p.op("vector", lambda v: v.reduce_max(sc[:, 0:1], hm_sb[:],
                                                   axis=AX.X),
                  inc="sv", waits=[elc])
        e2 = p.op("vector", lambda v: v.reduce_max(sc[:, 1:2], dm_sb[:],
                                                   axis=AX.X), inc="sv")
        e3 = p.op("vector", lambda v: v.tensor_scalar_max(sc[:, 0:2], sc[:, 0:2],
                                                          G_EPS),
                  inc="sv", waits=[e1, e2])
        e4 = p.op("vector", lambda v: v.reciprocal(sc[:, 2:4], sc[:, 0:2]),
                  inc="sv", waits=[e3])
        e5 = p.op("tensor", lambda t: t.matmul(ps_8[:], onesf8[:], sc[:, 2:4],
                                               start=True, stop=True),
                  inc="sp", waits=[e4, ec0])
        e6 = p.op("vector", lambda v: v.tensor_scalar_mul(ih8[:], ps_8[:], 0.5),
                  inc="sv", waits=[e5])
        e7 = p.op("vector", lambda v: v.tensor_scalar(
            out=logit[:], in0=h8_sb[:, 1:TCORE:2], scalar1=ih8[:, 0:1],
            scalar2=None, op0=ALU.mult), inc="sv", waits=[e6, elc])
        e8 = p.op("vector", lambda v: v.scalar_tensor_tensor(
            out=logit[:], in0=dsl_sb[:], scalar=ih8[:, 1:2], in1=logit[:],
            op0=ALU.mult, op1=ALU.add), inc="sv", waits=[e7])

        wn_evs = []
        for j in range(4):
            jt = p.op("tensor", lambda t, j=j: t.transpose(
                ps_tr[:], logit[:, j * 128:(j + 1) * 128],
                idf_sb[:]), inc="sp",
                waits=[e8, elc, wn_evs[-1] if wn_evs else None])
            g0 = p.op("vector", lambda v, j=j: v.tensor_copy(
                glt[:, j * NEXP:(j + 1) * NEXP], ps_tr[:]), inc="sv",
                waits=[jt])
            pr = glt[:, j * NEXP:(j + 1) * NEXP]
            g1 = p.op("vector", lambda v, pr=pr: v.reduce_max(
                gsc[:, 0:1], pr, axis=AX.X, negate=True), inc="sv", waits=[g0])
            g2 = p.op("scalar", lambda s, pr=pr: s.activation(
                out=probs[:], in_=pr, func=AF.Exp, bias=gsc[:, 0:1],
                accum_out=gsc[:, 1:2]), inc="sa", waits=[g1])
            g3 = p.op("vector", lambda v: v.reciprocal(gsc[:, 2:3], gsc[:, 1:2]),
                      inc="sv", waits=[g2])
            g4 = p.op("vector", lambda v: v.tensor_scalar_mul(
                probs[:], probs[:], gsc[:, 2:3]), inc="sv", waits=[g3])
            g5 = p.op("vector", lambda v: v.memset(trep[:], 1e30), inc="sv",
                      waits=[g4])
            g6 = p.op("vector", lambda v: v.reduce_max(trep[:, 0:1], probs[:],
                                                       axis=AX.X),
                      inc="sv", waits=[g5])
            g7 = p.op("vector", lambda v: v.match_replace(
                out=mrep[:], in_to_replace=trep[:], in_values=probs[:],
                imm_value=-1e30), inc="sv", waits=[g6])
            g8 = p.op("vector", lambda v: v.reduce_max(gsc[:, 3:4], mrep[:],
                                                       axis=AX.X),
                      inc="sv", waits=[g7])
            g9 = p.op("vector", lambda v: v.tensor_scalar_mul(
                gsc[:, 4:5], gsc[:, 3:4], -2.0), inc="sv", waits=[g8])
            g10 = p.op("scalar", lambda s: s.activation(
                out=sigm[:], in_=probs[:], func=AF.Sigmoid, scale=2.0,
                bias=gsc[:, 4:5]), inc="sa", waits=[g9])
            g11 = p.op("vector", lambda v: v.tensor_mul(wgt[:], probs[:],
                                                        sigm[:]),
                       inc="sv", waits=[g10])
            g12 = p.op("vector", lambda v: v.reduce_sum(gsc[:, 5:6], wgt[:],
                                                        axis=AX.X),
                       inc="sv", waits=[g11])
            g13 = p.op("vector", lambda v: v.tensor_scalar_add(
                gsc[:, 6:7], gsc[:, 5:6], 1e-9), inc="sv", waits=[g12])
            g14 = p.op("vector", lambda v: v.reciprocal(gsc[:, 7:8],
                                                        gsc[:, 6:7]),
                       inc="sv", waits=[g13])
            g15 = p.op("vector", lambda v: v.tensor_scalar_mul(
                wn[:], wgt[:], gsc[:, 7:8]), inc="sv", waits=[g14])
            g16 = p.op("tensor", lambda t: t.transpose(ps_tb[:], wn[:],
                                                       id_sb[:]),
                       inc="sp", waits=[g15])
            g17 = p.op("vector", lambda v, j=j: v.tensor_copy(
                wfm[:, j * 128:(j + 1) * 128], ps_tb[:]), inc="sv",
                waits=[g16])
            wn_evs.append(g17)
        ewf0 = p.op("vector", lambda v: v.tensor_copy(wf[:, 0:TCORE:2], wfm[:]),
                    inc="sv", waits=[wn_evs[-1]])
        ewf1 = p.op("vector", lambda v: v.tensor_copy(wf[:, 1:TCORE:2], wfm[:]),
                    inc="sv")
        zp_evs = []
        for t2 in range(NT2):
            epm = p.op("tensor", lambda t, t2=t2: t.matmul(
                ps_pm[:], limb[:], wf[:, t2 * T2:(t2 + 1) * T2],
                start=True, stop=True), inc="sp",
                waits=[ewf1, ecl, zp_evs[-1] if zp_evs else None])
            ezp = p.op("vector", lambda v, t2=t2: v.tensor_mul(
                zp[:, t2 * T2:(t2 + 1) * T2], z_sb[:, t2 * T2:(t2 + 1) * T2],
                ps_pm[:]), inc="sv", waits=[epm])
            zp_evs.append(ezp)

        cast_evs = []
        mm_evs = []
        ev_evs = []
        st_evs = []

        def wdma(o):
            b = o % 2
            gate = cast_evs[o - 2] if o >= 2 else None
            return p.dma("sync", wf32[b][:], w_in[o], f"wl{b}", waits=[gate])

        def rdma(o):
            b = o % 2
            gate = st_evs[o - 2] if o >= 2 else None
            return p.dma("sync", rt[b][:], r_in[o * 128:(o + 1) * 128, :],
                         f"rl{b}", waits=[gate])

        wd = [wdma(0), wdma(1)]
        rd = [rdma(0), rdma(1)]

        def cast_w(o):
            b = o % 2
            gate = mm_evs[o - 2] if o >= 2 else None
            ec = p.op("vector", lambda v, b=b: v.tensor_copy(wbb[b][:],
                                                             wf32[b][:]),
                      inc="sv", waits=[wd[o], gate, elc])
            cast_evs.append(ec)

        cast_w(0)
        for o in range(NC):
            b = o % 2
            ec = cast_evs[o]
            pgate = ev_evs[2 * (o - 2) + 1] if o >= 2 else None
            for k in range(NC):
                w0 = [ec] if k == 0 else []
                if k == 0 and pgate is not None:
                    w0.append(pgate)
                if k == 0:
                    w0 += [zp_evs[-1], ecb, ecr]
                p.op("tensor", lambda t, b=b, k=k: t.matmul(
                    ps_sw[b][0][:], wbb[b][:, k * 128:(k + 1) * 128],
                    xln_sb[:, (2 * k) * T2:(2 * k + 1) * T2],
                    start=(k == 0), stop=False), waits=w0)
                p.op("tensor", lambda t, b=b, k=k: t.matmul(
                    ps_sw[b][1][:], wbb[b][:, k * 128:(k + 1) * 128],
                    xln_sb[:, (2 * k + 1) * T2:(2 * k + 2) * T2],
                    start=(k == 0), stop=False))
            for j in range(2):
                p.op("tensor", lambda t, b=b, j=j, o=o: t.matmul(
                    ps_sw[b][j][:], bmh[:, o * 128:(o + 1) * 128],
                    zp[:, j * T2:(j + 1) * T2], start=False, stop=False))
            p.op("tensor", lambda t, b=b, o=o: t.matmul(
                ps_sw[b][0][:], br_b[:, o * 128:(o + 1) * 128], ones512[:],
                start=False, stop=True))
            em = p.op("tensor", lambda t, b=b, o=o: t.matmul(
                ps_sw[b][1][:], br_b[:, o * 128:(o + 1) * 128], ones512[:],
                start=False, stop=True), inc="sp")
            mm_evs.append(em)
            if o + 1 < NC:
                cast_w(o + 1)
            for j in range(2):
                eev = p.op("vector", lambda v, b=b, j=j: v.scalar_tensor_tensor(
                    out=rt[b][:, j * T2:(j + 1) * T2], in0=ps_sw[b][j][:],
                    scalar=0.5, in1=rt[b][:, j * T2:(j + 1) * T2],
                    op0=ALU.mult, op1=ALU.add),
                    inc="sv", waits=[em, rd[o]])
                ev_evs.append(eev)
            st_evs.append(
                p.dma("gpsimd", rout[o * 128:(o + 1) * 128, :], rt[b][:],
                      f"sb{b}", waits=[ev_evs[-1]]))
            if o + 2 < NC:
                wd.append(wdma(o + 2))
                rd.append(rdma(o + 2))
        p.wait("sync", p.ev("sb0"))
        p.wait("sync", p.ev("sb1"))
        p.emit()
    return nc


def host_L2_weights(W, b, Bmat, LiMEs):
    W_T = np.ascontiguousarray(W.T)
    return dict(w=tile4(W_T, NC, NC), bmat=np.ascontiguousarray(Bmat),
                brow=np.ascontiguousarray(b.reshape(1, D)),
                limes=np.ascontiguousarray(LiMEs), ident=IDENT_BF16,
                identf=np.eye(NEXP, dtype=np.float32))





def build_L3():
    nc = bass.Bass("TRN2", target_bir_lowering=False, debug=False)
    r_in = nc.dram_tensor("r", [D, TCORE], F32, kind="ExternalInput").ap()
    qc_in = nc.dram_tensor("qc", [NC, 128, NC * 128], I32, kind="ExternalInput").ap()
    kc_in = nc.dram_tensor("kc", [NKVH, 128, NC * 128], I32, kind="ExternalInput").ap()
    vc_in = nc.dram_tensor("vc", [NKVH, 128, NC * 128], I32, kind="ExternalInput").ap()
    qs_in = nc.dram_tensor("qs", [RANK, D], F32, kind="ExternalInput").ap()
    ks_in = nc.dram_tensor("ks", [RANK, DKV], F32, kind="ExternalInput").ap()
    vs_in = nc.dram_tensor("vs", [RANK, DKV], F32, kind="ExternalInput").ap()
    qp_in = nc.dram_tensor("qpos", [1, TCORE], F32, kind="ExternalInput").ap()
    if_in = nc.dram_tensor("invf", [1, 64], F32, kind="ExternalInput").ap()
    rq_in = nc.dram_tensor("redq", [64, TCORE], F32, kind="ExternalInput").ap()
    rc_in = nc.dram_tensor("redc", [64, TCORE], F32, kind="ExternalInput").ap()
    pr_in = nc.dram_tensor("prot", [128, 128], BF16, kind="ExternalInput").ap()
    id_in = nc.dram_tensor("ident", [128, 128], BF16, kind="ExternalInput").ap()
    q_out = nc.dram_tensor("q", [D, TCORE], BF16, kind="ExternalOutput").ap()
    k_out = nc.dram_tensor("ko", [DKV, TCORE], BF16, kind="ExternalOutput").ap()
    v_out = nc.dram_tensor("vo", [TCORE, DKV], BF16, kind="ExternalOutput").ap()
    qstage = nc.dram_tensor("qstage", [NC, RANK * 128], BF16).ap()
    kstage = nc.dram_tensor("kstage", [NKVH, RANK * 128], BF16).ap()
    vstage = nc.dram_tensor("vstage", [NKVH, RANK * 128], BF16).ap()
    wqc = nc.dram_tensor("wqc", [NC, 128, NC * 128], BF16).ap()
    wkc = nc.dram_tensor("wkc", [NKVH, 128, NC * 128], BF16).ap()
    wvc = nc.dram_tensor("wvc", [NKVH, 128, NC * 128], BF16).ap()

    with ExitStack() as stk:
        E = stk.enter_context
        p = Prog(nc, stk)
        B = ln_bufs(E, nc)
        scf = [E(nc.sbuf_tensor(f"scf{i}", [RANK, 128], F32)) for i in range(2)]
        scb = [E(nc.sbuf_tensor(f"scb{i}", [RANK, 128], BF16)) for i in range(2)]
        qp_sb = E(nc.sbuf_tensor("qp_sb", [1, TCORE], F32))
        invf = E(nc.sbuf_tensor("invf_sb", [1, 64], F32))
        rq_sb = E(nc.sbuf_tensor("rq_sb", [64, TCORE], F32))
        rc_sb = E(nc.sbuf_tensor("rc_sb", [64, TCORE], F32))
        pr_sb = E(nc.sbuf_tensor("pr_sb", [128, 128], BF16))
        id_sb = E(nc.sbuf_tensor("id_sb", [128, 128], BF16))
        cod = [E(nc.sbuf_tensor(f"cod{i}", [128, NC * 128], I32)) for i in range(2)]
        sbc = [E(nc.sbuf_tensor(f"sbc{i}", [128, NC * 128], BF16)) for i in range(2)]
        wdq = [E(nc.sbuf_tensor(f"wdq{i}", [128, NC * 128], BF16)) for i in range(2)]
        trg = E(nc.sbuf_tensor("trg", [64, 2 * T2], F32))
        cos2 = E(nc.sbuf_tensor("cos2", [128, T2], BF16))
        sin2 = E(nc.sbuf_tensor("sin2", [128, T2], BF16))
        raw = [E(nc.sbuf_tensor(f"raw{i}", [128, T2], BF16)) for i in range(2)]
        tmp = [E(nc.sbuf_tensor(f"tmp{i}", [128, T2], BF16)) for i in range(2)]
        fin = [E(nc.sbuf_tensor(f"fin{i}", [128, T2], BF16)) for i in range(2)]
        vtm = [E(nc.sbuf_tensor(f"vtm{i}", [128, 128], BF16)) for i in range(2)]
        ps_p = [E(nc.psum_tensor(f"ps_p{i}", [128, T2], F32)) for i in range(2)]
        ps_rot = E(nc.psum_tensor("ps_rot", [128, T2], F32))
        ps_tt = E(nc.psum_tensor("ps_tt", [128, 128], BF16))

        p.dma("sync", qp_sb[:], qp_in, "lc")
        p.dma("sync", invf[:], if_in, "lc")
        p.dma("sync", rq_sb[:], rq_in, "lc")
        p.dma("sync", rc_sb[:], rc_in, "lc")
        p.dma("sync", pr_sb[:], pr_in, "lc")
        p.dma("sync", id_sb[:], id_in, "lc")
        elc = p.ev("lc")
        emit_ln_consts(p, B)
        ei = elc
        sc_cast = [None, None]
        sc_st = [None, None]
        jobs = ([(qs_in, qstage, o) for o in range(NC)]
                + [(ks_in, kstage, o) for o in range(NKVH)]
                + [(vs_in, vstage, o) for o in range(NKVH)])
        for n, (src_t, dst_t, o) in enumerate(jobs):
            par = n % 2
            eld = p.dma("sync", scf[par][:], src_t[:, o * 128:(o + 1) * 128],
                        f"sl{par}", waits=[sc_cast[par]])
            ecst = p.op("vector", lambda v, par=par: v.tensor_scalar_mul(
                scb[par][:], scf[par][:], 0.5), inc="sv",
                waits=[eld, sc_st[par]])
            sc_cast[par] = ecst
            sc_st[par] = p.dma(
                "gpsimd",
                dst_t[o:o + 1, :].rearrange("o (a b) -> (o a) b", a=RANK),
                scb[par][:], f"stg{par}", waits=[ecst])
        estg0, estg1 = p.ev("stg0"), p.ev("stg1")

        deq_ev = [None, None]
        mm_ev = [None, None]
        rawrd_ev = [None, None]
        finst_ev = [None, None]
        vst_ev = [None, None]
        last_fin = [None]
        last_vtc = [None]

        wst_ev = [None, None]

        def proj_chunk(codes_src, stage_src, par, cache_dst=None,
                       cache_src=None):
            gate = mm_ev[par]
            if cache_src is not None:
                ed = p.dma("sync", wdq[par][:], cache_src, f"c{par}",
                           waits=[gate, wst_ev[par]])
                deq_ev[par] = ed
                return ed
            ec = p.dma("sync", cod[par][:], codes_src, f"c{par}", waits=[gate])
            ebc = p.dma(
                "sync", sbc[par][:],
                stage_src.broadcast_to([128, NC * 128]), f"b{par}",
                waits=[estg0, estg1])
            ed = p.op("vector", lambda v, par=par: v.scalar_tensor_tensor(
                out=wdq[par][:], in0=cod[par][:], scalar=8.0, in1=sbc[par][:],
                op0=ALU.subtract, op1=ALU.mult), inc="sv",
                waits=[ec, ebc, gate, wst_ev[par]])
            deq_ev[par] = ed
            if cache_dst is not None:
                wst_ev[par] = p.dma("gpsimd", cache_dst, wdq[par][:],
                                    f"wc{par}", waits=[ed])
            return ed

        def mms(par, extra):
            ev = None
            for k in range(NC):
                w = ([deq_ev[par], rawrd_ev[par]] + extra) if k == 0 else []
                ev = p.op("tensor", lambda t, par=par, k=k: t.matmul(
                    ps_p[par][:], wdq[par][:, k * 128:(k + 1) * 128],
                    B["xb"][:, k * T2:(k + 1) * T2],
                    start=(k == 0), stop=(k == NC - 1)),
                    inc="sp" if k == NC - 1 else None, waits=w)
            mm_ev[par] = ev
            return ev

        def rope_store(par, dst, etab):
            e_raw = p.op("vector", lambda v, par=par: v.tensor_copy(
                raw[par][:], ps_p[par][:]), inc="sv",
                waits=[mm_ev[par], rawrd_ev[par]])
            e_rot = p.op("tensor", lambda t, par=par: t.matmul(
                ps_rot[:], pr_sb[:], raw[par][:], start=True, stop=True),
                inc="sp", waits=[e_raw, elc, last_fin[0]])
            e_tt = p.op("vector", lambda v, par=par: v.tensor_mul(
                tmp[par][:], raw[par][:], cos2[:]), inc="sv",
                waits=[etab, e_raw])
            e_fin = p.op("vector", lambda v, par=par: v.scalar_tensor_tensor(
                out=fin[par][:], in0=ps_rot[:], scalar=1.0, in1=sin2[:],
                op0=ALU.mult, op1=ALU.mult), inc="sv",
                waits=[e_rot, etab, finst_ev[par]])
            e_fin2 = p.op("vector", lambda v, par=par: v.tensor_add(
                fin[par][:], fin[par][:], tmp[par][:]), inc="sv",
                waits=[e_tt, e_fin])
            last_fin[0] = e_fin
            rawrd_ev[par] = e_tt
            est = p.dma("gpsimd", dst, fin[par][:], f"s{par}", waits=[e_fin2])
            finst_ev[par] = est
            return est

        prev = {}
        for t2 in range(NT2):
            for c in range(NC):
                p.dma("sync", B["rbuf"][:, c * T2:(c + 1) * T2],
                      r_in[c * 128:(c + 1) * 128, t2 * T2:(t2 + 1) * T2], "ld",
                      waits=[prev.get("pool")] if c == 0 else ())
            r_ready = p.ev("ld")
            xln_evs, done = emit_layernorm(p, B, r_ready, prev)
            xln_all = xln_evs[-1]

            ang = B["ps_b"][0:64, :]
            s_in = trg[:, 0:T2]
            c_in = trg[:, T2:2 * T2]
            tsl = slice(t2 * T2, (t2 + 1) * T2)
            er0 = p.op("tensor", lambda t, t2=t2: t.matmul(
                ang, invf[:], qp_sb[:, t2 * T2:(t2 + 1) * T2],
                start=True, stop=True), inc="sp", waits=[ei, xln_all])
            er4 = p.op("vector", lambda v, tsl=tsl: v.scalar_tensor_tensor(
                out=s_in, in0=rq_sb[:, tsl], scalar=-2 * PI, in1=ang,
                op0=ALU.mult, op1=ALU.add), inc="sv",
                waits=[er0, prev.get("rope")])
            er5 = p.op("vector", lambda v: v.tensor_scalar(
                out=s_in, in0=s_in, scalar1=PI, scalar2=-PI,
                op0=ALU.min, op1=ALU.max), inc="sv", waits=[er4])
            ec4 = p.op("vector", lambda v, tsl=tsl: v.scalar_tensor_tensor(
                out=c_in, in0=rc_sb[:, tsl], scalar=-2 * PI, in1=ang,
                op0=ALU.mult, op1=ALU.add), inc="sv", waits=[er0])
            ec5 = p.op("vector", lambda v: v.tensor_scalar(
                out=c_in, in0=c_in, scalar1=PI / 2, scalar2=None,
                op0=ALU.add), inc="sv", waits=[ec4])
            ec6 = p.op("vector", lambda v: v.tensor_scalar(
                out=c_in, in0=c_in, scalar1=PI, scalar2=-PI,
                op0=ALU.min, op1=ALU.max), inc="sv", waits=[ec5])
            eb1 = p.op("scalar", lambda s: s.activation(
                out=sin2[0:64, :], in_=s_in, func=AF.Sin, scale=-1.0),
                inc="sa", waits=[er5, prev.get("rope")])
            eb2 = p.op("scalar", lambda s: s.activation(
                out=sin2[64:128, :], in_=s_in, func=AF.Sin), inc="sa")
            eb3 = p.op("scalar", lambda s: s.activation(
                out=cos2[0:64, :], in_=c_in, func=AF.Sin), inc="sa",
                waits=[ec6])
            eb4 = p.op("scalar", lambda s: s.activation(
                out=cos2[64:128, :], in_=c_in, func=AF.Sin), inc="sa")
            etab = eb4

            jobs = []
            for o in range(NKVH):
                jobs.append(("k", o, kc_in[o], kstage[o:o + 1, :], wkc[o]))
            for o in range(NKVH):
                jobs.append(("v", o, vc_in[o], vstage[o:o + 1, :], wvc[o]))
            for o in range(NC):
                jobs.append(("q", o, qc_in[o], qstage[o:o + 1, :], wqc[o]))

            def dq(i):
                kind, o, csrc, ssrc, wc = jobs[i]
                proj_chunk(csrc, ssrc, i % 2,
                           cache_dst=wc if t2 == 0 else None,
                           cache_src=wc if t2 == 1 else None)

            dq(0)
            dq(1)
            for i, (kind, o, csrc, ssrc, wc) in enumerate(jobs):
                par = i % 2
                mms(par, [xln_all])
                if i + 2 < len(jobs):
                    dq(i + 2)
                if kind == "k":
                    rope_store(par, k_out[o * 128:(o + 1) * 128,
                                          t2 * T2:(t2 + 1) * T2], etab)
                elif kind == "q":
                    rope_store(par, q_out[o * 128:(o + 1) * 128,
                                          t2 * T2:(t2 + 1) * T2], etab)
                else:
                    e_raw = p.op("vector", lambda v, par=par: v.tensor_copy(
                        raw[par][:], ps_p[par][:]), inc="sv",
                        waits=[mm_ev[par], rawrd_ev[par]])
                    etr_last = None
                    for ii in range(4):
                        etr = p.op("tensor", lambda t, par=par, ii=ii:
                                   t.transpose(
                                       ps_tt[:], raw[par][:, ii * 128:(ii + 1) * 128],
                                       id_sb[:]), inc="sp",
                                   waits=[e_raw, elc, last_vtc[0]])
                        vpar = ii % 2
                        evc = p.op("vector", lambda v, vpar=vpar: v.tensor_copy(
                            vtm[vpar][:], ps_tt[:]), inc="sv",
                            waits=[etr, vst_ev[vpar]])
                        last_vtc[0] = evc
                        est = p.dma("gpsimd",
                                    v_out[t2 * T2 + ii * 128:t2 * T2 + (ii + 1) * 128,
                                          o * 128:(o + 1) * 128],
                                    vtm[vpar][:], f"v{vpar}", waits=[evc])
                        vst_ev[vpar] = est
                        etr_last = etr
                    rawrd_ev[par] = etr_last

            prev = dict(done)
            prev["pe"] = p.ev("sp")
            prev["dve"] = p.ev("sv")
            prev["rope"] = p.ev("sv")
        for s in ("s0", "s1", "v0", "v1"):
            p.wait("sync", p.ev(s))
        p.emit()
    return nc


def host_rope(tok):
    invf = (1.0 / THETA ** (np.arange(64) / 64.0)).astype(np.float32)
    pos = tok.astype(np.float32)
    ang = invf[:, None] * pos[None, :]
    redq = np.round(ang / np.float32(2 * PI)).astype(np.float32)
    redc = np.round((ang + np.float32(PI / 2)) / np.float32(2 * PI)).astype(np.float32)
    return dict(invf=invf.reshape(1, 64), redq=np.ascontiguousarray(redq),
                redc=np.ascontiguousarray(redc))


def host_L3_weights(q_codes, k_codes, v_codes, q_scales, k_scales, v_scales):
    return dict(
        qc=tile4(q_codes, NC, NC), kc=tile4(k_codes, NC, NKVH),
        vc=tile4(v_codes, NC, NKVH),
        qs=np.ascontiguousarray(q_scales), ks=np.ascontiguousarray(k_scales),
        vs=np.ascontiguousarray(v_scales),
        prot=np.ascontiguousarray(
            np.roll(np.eye(128, dtype=np.float32), 64, axis=1).astype(BF16NP)),
        ident=IDENT_BF16)





def build_L4():
    nc = bass.Bass("TRN2", target_bir_lowering=False, debug=False)
    r_in = nc.dram_tensor("r", [D, TCORE], F32, kind="ExternalInput").ap()
    q_in = nc.dram_tensor("q", [D, TCORE], BF16, kind="ExternalInput").ap()
    kg_in = nc.dram_tensor("kg", [DKV, SEQ], BF16, kind="ExternalInput").ap()
    vg_in = nc.dram_tensor("vg", [NKVH, SEQ, DH], BF16, kind="ExternalInput").ap()
    oc_in = nc.dram_tensor("oc", [NC, 128, NC * 128], I32, kind="ExternalInput").ap()
    os_in = nc.dram_tensor("osc", [RANK, D], F32, kind="ExternalInput").ap()
    qp_in = nc.dram_tensor("qpos", [1, TCORE], F32, kind="ExternalInput").ap()
    ia_in = nc.dram_tensor("iap", [128, 1], F32, kind="ExternalInput").ap()
    rout = nc.dram_tensor("rout", [D, TCORE], F32, kind="ExternalOutput").ap()
    ostage = nc.dram_tensor("ostage", [NC, RANK * 128], BF16).ap()

    TQ = 256
    NJ = 4

    with ExitStack() as stk:
        E = stk.enter_context
        p = Prog(nc, stk)
        qsb = E(nc.sbuf_tensor("qsb", [128, NC * TQ], BF16))
        att = E(nc.sbuf_tensor("att", [128, NC * TCORE], BF16))
        ksb = E(nc.sbuf_tensor("ksb", [128, 16 * 128], BF16))
        vsb = E(nc.sbuf_tensor("vsb", [128, 16 * 128], BF16))
        prb = [E(nc.sbuf_tensor(f"prb{i}", [128, 4 * TQ], BF16)) for i in range(2)]
        msk = E(nc.sbuf_tensor("msk", [128, 4 * TQ], BF16))
        mskf = E(nc.sbuf_tensor("mskf", [128, TQ], F32))
        qpb = E(nc.sbuf_tensor("qpb", [128, TQ], F32))
        ia_sb = E(nc.sbuf_tensor("ia_sb", [128, 1], F32))
        ones1 = E(nc.sbuf_tensor("ones1", [128, 1], BF16))
        onesf = E(nc.sbuf_tensor("onesf", [1, 128], F32))
        srow = E(nc.sbuf_tensor("srow", [1, TQ], F32))
        sbf = [E(nc.sbuf_tensor(f"sbf{i}", [128, TQ], F32)) for i in range(2)]
        scf = [E(nc.sbuf_tensor(f"scf{i}", [RANK, 128], F32)) for i in range(2)]
        scb = [E(nc.sbuf_tensor(f"scb{i}", [RANK, 128], BF16)) for i in range(2)]
        cod = [E(nc.sbuf_tensor(f"cod{i}", [128, NC * 128], I32)) for i in range(2)]
        sbc = [E(nc.sbuf_tensor(f"sbc{i}", [128, NC * 128], BF16)) for i in range(2)]
        wdq = [E(nc.sbuf_tensor(f"wdq{i}", [128, NC * 128], BF16)) for i in range(2)]
        rt = [E(nc.sbuf_tensor(f"rt_{i}", [128, TCORE], F32)) for i in range(2)]

        ps_sc = E(nc.psum_tensor("ps_sc", [128, 4 * TQ], F32))
        ps_at = [E(nc.psum_tensor(f"ps_at{i}", [128, TQ], F32)) for i in range(2)]
        ps_S = E(nc.psum_tensor("ps_S", [1, TQ], F32))
        ps_bc = E(nc.psum_tensor("ps_bc", [128, TQ], F32))
        ps_o = [E(nc.psum_tensor(f"ps_o{i}", [128, T2], F32)) for i in range(2)]

        p.dma("sync", ia_sb[:], ia_in, "lc")
        elc = p.ev("lc")
        ec1 = p.op("vector", lambda v: v.memset(ones1[:], 1.0), inc="sv")
        ec2 = p.op("vector", lambda v: v.memset(onesf[:], 1.0), inc="sv")
        ecst = [None, None]
        est_ = [None, None]
        for n in range(NC):
            par = n % 2
            eld = p.dma("sync", scf[par][:], os_in[:, n * 128:(n + 1) * 128],
                        f"sl{par}", waits=[ecst[par]])
            ee = p.op("vector", lambda v, par=par: v.tensor_scalar_mul(
                scb[par][:], scf[par][:], 0.5), inc="sv",
                waits=[eld, est_[par]])
            ecst[par] = ee
            est_[par] = p.dma(
                "gpsimd",
                ostage[n:n + 1, :].rearrange("o (a b) -> (o a) b", a=RANK),
                scb[par][:], f"stg{par}", waits=[ee])
        estg0, estg1 = p.ev("stg0"), p.ev("stg1")

        att_ev = []
        kld_prev = None
        vld_prev = None
        for j in range(NJ):
            nch = 4 * j + 4
            qgate = att_ev[-1] if att_ev else None
            for c in range(NC):
                p.dma("sync", qsb[:, c * TQ:(c + 1) * TQ],
                      q_in[c * 128:(c + 1) * 128, j * TQ:(j + 1) * TQ], "lq",
                      waits=[qgate] if c == 0 else ())
            eq = p.ev("lq")
            ebq = p.dma("sync", qpb[:],
                        qp_in[0:1, j * TQ:(j + 1) * TQ].broadcast_to([128, TQ]),
                        "lqb", waits=[att_ev[-1] if att_ev else None])
            mk_evs = []
            for mi in range(4):
                ci = nch - 4 + mi
                e1_ = p.op("vector", lambda v, ci=ci: v.tensor_scalar(
                    out=mskf[:], in0=qpb[:], scalar1=ia_sb[:, 0:1],
                    scalar2=float(128 * ci) - 0.5, op0=ALU.subtract,
                    op1=ALU.subtract), inc="sv",
                    waits=[ebq, elc, mk_evs[-1] if mi else att_ev[-1] if att_ev else None])
                e2_ = p.op("scalar", lambda s: s.activation(
                    out=mskf[:], in_=mskf[:], func=AF.Sign), inc="sa",
                    waits=[e1_])
                e3_ = p.op("vector", lambda v, mi=mi: v.tensor_scalar(
                    out=msk[:, mi * TQ:(mi + 1) * TQ], in0=mskf[:], scalar1=0.5,
                    scalar2=0.5, op0=ALU.mult, op1=ALU.add), inc="sv",
                    waits=[e2_])
                mk_evs.append(e3_)
            emsk = mk_evs[-1]

            for h in range(NKVH):
                ek = p.dma("sync", ksb[:, 0:nch * 128],
                           kg_in[h * 128:(h + 1) * 128, 0:nch * 128], "lk",
                           waits=[kld_prev])
                ev_ = []
                for ci in range(nch):
                    ev_.append(p.dma(
                        "sync", vsb[:, ci * 128:(ci + 1) * 128],
                        vg_in[h, ci * 128:(ci + 1) * 128, :], "lv",
                        waits=[vld_prev] if ci == 0 else ()))
                ekv = (p.ev("lk"), p.ev("lv"))
                for hq in range(4 * h, 4 * h + 4):
                    qh = hq
                    ngr = (nch + 3) // 4
                    exp_evs = []
                    spv_evs = []
                    apar = qh % 2
                    aslot = j % 2
                    for g in range(ngr):
                        c0 = 4 * g
                        c1 = min(nch, c0 + 4)
                        par = g % 2
                        for ci in range(c0, c1):
                            w = []
                            if ci == 0:
                                w = [ekv[0], ekv[1], eq,
                                     att_ev[-1] if att_ev else None]
                            if ci == c0:
                                w.append(exp_evs[-1] if exp_evs else
                                         (att_ev[-1] if att_ev else None))
                            p.op("tensor", lambda t, ci=ci, qh=qh: t.matmul(
                                ps_sc[:, (ci % 4) * TQ:(ci % 4 + 1) * TQ],
                                ksb[:, ci * 128:(ci + 1) * 128],
                                qsb[:, qh * TQ:(qh + 1) * TQ],
                                start=True, stop=True),
                                inc="sp" if ci == c1 - 1 else None, waits=w)
                        eqk = p.ev("sp")
                        wex = [eqk]
                        if len(spv_evs) >= 2:
                            wex.append(spv_evs[-2])
                        elif not spv_evs and att_ev:
                            wex.append(att_ev[-1])
                        ee = p.op("scalar", lambda s, c0=c0, c1=c1, par=par:
                                  s.activation(
                                      out=prb[par][:, 0:(c1 - c0) * TQ],
                                      in_=ps_sc[:, 0:(c1 - c0) * TQ],
                                      func=AF.Exp,
                                      scale=float(1.0 / np.sqrt(DH))),
                                  inc="sa", waits=wex)
                        exp_evs.append(ee)
                        emg = ee
                        for ci in range(max(c0, nch - 4), c1):
                            mi = ci - (nch - 4)
                            emg = p.op("vector", lambda v, ci=ci, par=par, mi=mi:
                                       v.tensor_mul(
                                           prb[par][:, (ci % 4) * TQ:
                                                    (ci % 4 + 1) * TQ],
                                           prb[par][:, (ci % 4) * TQ:
                                                    (ci % 4 + 1) * TQ],
                                           msk[:, mi * TQ:(mi + 1) * TQ]),
                                       inc="sv", waits=[emg, emsk])
                        ela = None
                        for ci in range(c0, c1):
                            w = [emg] if ci == c0 else []
                            if ci == 0:
                                w.append(("sv", p.cnt["sv"]))
                            p.op("tensor", lambda t, ci=ci, par=par, nch=nch:
                                 t.matmul(
                                     ps_S[:], ones1[:],
                                     prb[par][:, (ci % 4) * TQ:
                                              (ci % 4 + 1) * TQ],
                                     start=(ci == 0), stop=(ci == nch - 1)),
                                 inc="sp" if ci == nch - 1 else None,
                                 waits=w)
                            ela = p.op("tensor", lambda t, ci=ci, par=par,
                                       apar=apar, nch=nch: t.matmul(
                                           ps_at[apar][:],
                                           vsb[:, ci * 128:(ci + 1) * 128],
                                           prb[par][:, (ci % 4) * TQ:
                                                    (ci % 4 + 1) * TQ],
                                           start=(ci == 0),
                                           stop=(ci == nch - 1)),
                                       inc="sp" if ci == nch - 1 else None)
                        spv_evs.append(ela if ela is not None else p.ev("sp"))
                    epv = p.ev("sp")
                    es1 = p.op("vector", lambda v: v.tensor_copy(
                        srow[:], ps_S[:]), inc="sv", waits=[epv])
                    es2 = p.op("vector", lambda v: v.reciprocal(
                        srow[:], srow[:]), inc="sv", waits=[es1])
                    es3 = p.op("tensor", lambda t: t.matmul(
                        ps_bc[:], onesf[:], srow[:],
                        start=True, stop=True), inc="sp", waits=[es2, ec2])
                    es4 = p.op("vector", lambda v, apar=apar: v.tensor_copy(
                        sbf[apar][:], ps_bc[:]), inc="sv", waits=[es3])
                    es5 = p.op("vector", lambda v, qh=qh, j=j, apar=apar:
                               v.scalar_tensor_tensor(
                                   out=att[:, qh * TCORE + j * TQ:
                                           qh * TCORE + (j + 1) * TQ],
                                   in0=ps_at[apar][:], scalar=1.0,
                                   in1=sbf[apar][:], op0=ALU.mult,
                                   op1=ALU.mult), inc="sv", waits=[es4])
                    att_ev.append(es5)
                kld_prev = p.ev("sv")
                vld_prev = p.ev("sv")

        deq_ev = [None, None]
        mm_ev = [None, None]
        st_evs = []
        rd = []

        def wdma(o):
            par = o % 2
            gate = deq_ev[par]
            ec = p.dma("sync", cod[par][:], oc_in[o], f"c{par}", waits=[gate])
            ebc = p.dma("sync", sbc[par][:],
                        ostage[o:o + 1, :].broadcast_to([128, NC * 128]),
                        f"b{par}", waits=[estg0, estg1])
            return (ec, ebc)

        def rdma(o):
            b = o % 2
            gate = st_evs[o - 2] if o >= 2 else None
            return p.dma("sync", rt[b][:], r_in[o * 128:(o + 1) * 128, :],
                         f"rl{b}", waits=[gate])

        pre = [wdma(0), wdma(1)]
        rd = [rdma(0), rdma(1)]
        ev_evs = []
        ed_list = []

        def deq_o(o):
            par = o % 2
            ec, ebc = pre[o]
            gate = mm_ev[par]
            ed = p.op("vector", lambda v, par=par: v.scalar_tensor_tensor(
                out=wdq[par][:], in0=cod[par][:], scalar=8.0, in1=sbc[par][:],
                op0=ALU.subtract, op1=ALU.mult), inc="sv",
                waits=[ec, ebc, gate])
            deq_ev[par] = ed
            ed_list.append(ed)

        deq_o(0)
        for o in range(NC):
            par = o % 2
            ed = ed_list[o]
            em = None
            for k in range(NC):
                w = [ed, att_ev[-1]] if k == 0 else []
                if k == 0 and o >= 1:
                    w.append(ev_evs[2 * (o - 1) + 1])
                for th in range(2):
                    em = p.op("tensor", lambda t, par=par, k=k, th=th: t.matmul(
                        ps_o[th][:], wdq[par][:, k * 128:(k + 1) * 128],
                        att[:, k * TCORE + th * T2:k * TCORE + (th + 1) * T2],
                        start=(k == 0), stop=(k == NC - 1)),
                        inc="sp" if (k == NC - 1 and th == 1) else None,
                        waits=w if th == 0 else ())
            mm_ev[par] = em
            if o + 1 < NC:
                deq_o(o + 1)
            for th in range(2):
                eev = p.op("vector", lambda v, par=par, th=th:
                           v.scalar_tensor_tensor(
                               out=rt[par][:, th * T2:(th + 1) * T2],
                               in0=ps_o[th][:], scalar=1.0,
                               in1=rt[par][:, th * T2:(th + 1) * T2],
                               op0=ALU.mult, op1=ALU.add),
                           inc="sv", waits=[em, rd[o]])
                ev_evs.append(eev)
            st_evs.append(
                p.dma("gpsimd", rout[o * 128:(o + 1) * 128, :], rt[par][:],
                      f"sb{par}", waits=[ev_evs[-1]]))
            if o + 2 < NC:
                pre.append(wdma(o + 2))
                rd.append(rdma(o + 2))
        p.wait("sync", p.ev("sb0"))
        p.wait("sync", p.ev("sb1"))
        p.emit()
    return nc


def host_L4_fixed(o_codes, o_scales):
    return dict(oc=tile4(o_codes, NC, NC), osc=np.ascontiguousarray(o_scales),
                iap=np.arange(128, dtype=np.float32).reshape(128, 1))


def host_kv_glue(ko_pair, vo_pair):
    kg = np.empty((DKV, SEQ), dtype=BF16NP)
    vg = np.empty((NKVH, SEQ, DH), dtype=BF16NP)
    for g in range(16):
        pcore = (g // 2) % 2
        slot = 2 * (g // 4) + (g % 2)
        kg[:, g * 128:(g + 1) * 128] = ko_pair[pcore][:, slot * 128:(slot + 1) * 128]
        for h in range(NKVH):
            vg[h, g * 128:(g + 1) * 128, :] = \
                vo_pair[pcore][slot * 128:(slot + 1) * 128, h * 128:(h + 1) * 128]
    return kg, vg


def kernel(L0_attn_x, L0_attn_q_codes, L0_attn_k_codes, L0_attn_v_codes,
           L0_attn_o_codes, L0_attn_q_scales, L0_attn_k_scales,
           L0_attn_v_scales, L0_attn_o_scales, L0_ffn_W, L0_ffn_b, L0_ffn_A,
           L0_ffn_Bmat, L0_ffn_LiMEs):
    x = np.asarray(L0_attn_x)
    cores = list(range(NCORES))
    toks = [core_token_idx(c) for c in cores]

    nc1 = build_L1()
    nc2 = build_L2()
    nc3 = build_L3()
    nc4 = build_L4()

    w1 = host_L1_weights(L0_ffn_W, L0_ffn_b, L0_ffn_A, L0_ffn_LiMEs)
    w2 = host_L2_weights(L0_ffn_W, L0_ffn_b, L0_ffn_Bmat, L0_ffn_LiMEs)
    w3 = host_L3_weights(L0_attn_q_codes, L0_attn_k_codes, L0_attn_v_codes,
                         L0_attn_q_scales, L0_attn_k_scales, L0_attn_v_scales)
    w4 = host_L4_fixed(L0_attn_o_codes, L0_attn_o_scales)
    rope = {0: host_rope(toks[0]), 1: host_rope(toks[1])}
    qpos = {par: toks[par].astype(np.float32).reshape(1, TCORE)
            for par in (0, 1)}

    def ffn_launches(rs):
        m1 = []
        for c in cores:
            m = dict(w1)
            m["r"] = rs[c]
            m1.append(m)
        r1 = run_bass_kernel_spmd(nc1, m1, core_ids=cores)
        hmax8 = np.ascontiguousarray(np.array(
            [[r1.results[c]["lmax"][0, 0] for c in cores]], np.float32))
        dmax8 = np.ascontiguousarray(np.array(
            [[r1.results[c]["lmax"][0, 1] for c in cores]], np.float32))
        m2 = []
        for c in cores:
            o = r1.results[c]
            m = dict(w2)
            m.update(r=rs[c], xln=o["xln"], z=o["z"], h8=o["h8"], dsl=o["dsl"],
                     hmax8=hmax8, dmax8=dmax8)
            m2.append(m)
        r2 = run_bass_kernel_spmd(nc2, m2, core_ids=cores)
        return [r2.results[c]["rout"] for c in cores]

    rs = [np.ascontiguousarray(x[c // 2, toks[c]].T.astype(np.float32))
          for c in cores]
    rs = ffn_launches(rs)
    m3 = []
    for c in cores:
        m = dict(w3)
        m.update(r=rs[c], qpos=qpos[c % 2])
        m.update(rope[c % 2])
        m3.append(m)
    r3 = run_bass_kernel_spmd(nc3, m3, core_ids=cores)
    m4 = []
    for c in cores:
        b = c // 2
        kg, vg = host_kv_glue(
            [r3.results[2 * b]["ko"], r3.results[2 * b + 1]["ko"]],
            [r3.results[2 * b]["vo"], r3.results[2 * b + 1]["vo"]])
        m = dict(w4)
        m.update(r=rs[c], q=r3.results[c]["q"], kg=kg, vg=vg,
                 qpos=qpos[c % 2])
        m4.append(m)
    r4 = run_bass_kernel_spmd(nc4, m4, core_ids=cores)
    rs = [r4.results[c]["rout"] for c in cores]
    rs = ffn_launches(rs)

    out = np.empty((x.shape[0], SEQ, D), np.float32)
    for c in cores:
        out[c // 2, toks[c]] = rs[c].T
    return out
